# version 03
# baseline (speedup 1.0000x reference)
import numpy as np
import ml_dtypes
import concourse.bass as bass
import concourse.mybir as mybir
from concourse.bass_utils import run_bass_kernel_spmd
from contextlib import ExitStack

F32 = mybir.dt.float32
BF16 = mybir.dt.bfloat16
AF = mybir.ActivationFunctionType
ALU = mybir.AluOpType
AX = mybir.AxisListType

D = 1024
NB = 8192
T = 4096
TG = 512
NG = T // TG
NCH = NB // 128
DCH = D // 128
E = 8
SU = 4
NU = NCH // SU

ENGINES = ("tensor", "vector", "scalar", "sync", "gpsimd")


class Sem:
    def __init__(self, handle, name):
        self.h = handle
        self.name = name
        self.n = 0


_HAZ_READERS = {"tensor_copy", "tensor_scalar", "tensor_scalar_add",
                "tensor_scalar_sub", "tensor_scalar_mul", "tensor_scalar_max",
                "tensor_scalar_min", "reciprocal", "reduce_max", "reduce_sum",
                "tensor_reduce"}


def _ap_name(a):
    try:
        return a.tensor.name
    except AttributeError:
        return None


class Ctx:
    def __init__(self):
        self.prog = {e: [] for e in ENGINES}
        self.mark = {e: {} for e in ENGINES}
        self._last_vec_write = None

    def op(self, eng, method, *args, waits=(), inc=None, incn=1, **kw):
        if eng == "vector" and method is not None:
            if self._last_vec_write is not None:
                names = {_ap_name(a) for a in args} - {None}
                for s1, s2 in (kw.get("waits") or ()):
                    pass
                if self._last_vec_write in names:
                    self.prog["vector"].append(lambda E: E.engine_nop())
            self._last_vec_write = _ap_name(args[0]) if args else None
        wl = []
        for s, v in waits:
            if v <= 0:
                continue
            assert v <= s.n, f"wait on future {s.name}: {v} > {s.n} ({eng} {method})"
            if self.mark[eng].get(s.name, 0) < v:
                wl.append((s.h, v))
                self.mark[eng][s.name] = v
        if inc is not None:
            inc.n += incn

        def run(Eng, method=method, args=args, kw=kw, wl=wl, inc=inc, incn=incn):
            for h, v in wl:
                Eng.wait_ge(h, v)
            if method is None:
                return
            r = getattr(Eng, method)(*args, **kw)
            if inc is not None:
                r.then_inc(inc.h, incn)

        self.prog[eng].append(run)


_PHASES = "ABC"
_DEBUG_IO = False


def build_nc(phases=None):
    phases = phases or _PHASES
    nc = bass.Bass("TRN2", target_bir_lowering=False)

    xT = nc.dram_tensor("xT", [D, T], F32, kind="ExternalInput").ap()
    KT = nc.dram_tensor("KT", [D, NB], F32, kind="ExternalInput").ap()
    V = nc.dram_tensor("V", [NB, D], F32, kind="ExternalInput").ap()
    Wu = nc.dram_tensor("Wu", [E, D, 256], F32, kind="ExternalInput").ap()
    Wv = nc.dram_tensor("Wv", [E, D, 256], F32, kind="ExternalInput").ap()
    Wo = nc.dram_tensor("Wo", [E, 256, D], F32, kind="ExternalInput").ap()
    gw = nc.dram_tensor("gw", [D, E], F32, kind="ExternalInput").ap()
    wmod = nc.dram_tensor("wmod", [D, 6 * D], F32, kind="ExternalInput").ap()
    cv = nc.dram_tensor("cv", [D], F32, kind="ExternalInput").ap()
    idf = nc.dram_tensor("idf", [8, 8], F32, kind="ExternalInput").ap()
    idb = nc.dram_tensor("idb", [128, 128], F32, kind="ExternalInput").ap()
    selb = nc.dram_tensor("selb", [8, 1024], BF16, kind="ExternalInput").ap()

    yT = nc.dram_tensor("yT", [D, T], F32, kind="ExternalOutput").ap()
    skind = "ExternalOutput" if _DEBUG_IO else "Internal"
    Sdram = nc.dram_tensor("Sdram", [128, NG, NCH, TG], BF16, kind=skind).ap()
    X2dram = nc.dram_tensor("X2dram", [128, NG, DCH, TG], F32, kind=skind).ap()
    if _DEBUG_IO:
        dbgM = nc.dram_tensor("dbgM", [128, 48], F32, kind="ExternalOutput").ap()
        dbgH = nc.dram_tensor("dbgH", [128, 4096], BF16, kind="ExternalOutput").ap()
        dbgC = nc.dram_tensor("dbgC", [128, 8192], BF16, kind="ExternalOutput").ap()
        dbgT = nc.dram_tensor("dbgT", [8, 512], BF16, kind="ExternalOutput").ap()
        dbgB = nc.dram_tensor("dbgB", [128, 512], BF16, kind="ExternalOutput").ap()
        dbgS = nc.dram_tensor("dbgS", [128, 4096], BF16, kind="ExternalOutput").ap()
        dbgX = nc.dram_tensor("dbgX", [128, 8192], F32, kind="ExternalOutput").ap()

    C = Ctx()

    with ExitStack() as st:
        sb = lambda *a: st.enter_context(nc.sbuf_tensor(*a))
        psf = lambda *a: st.enter_context(nc.psum_tensor(*a))

        wbig = sb("wbig", [128, 65536], BF16)
        xg = sb("xg", [128, 8192], F32)
        hg = sb("hg", [128, 4096], BF16)
        sbS = sb("sbS", [128, 4096], BF16)
        su = sb("su", [128, 1024], BF16)
        S2 = sb("S2", [128, 1024], BF16)
        scb = sb("scb", [128, 8192], BF16)
        rowlg = sb("rowlg", [8, 1024], F32)
        rowB = sb("rowB", [1, 1024], BF16)
        modv = sb("modv", [128, 48], F32)
        csc = sb("csc", [128, 16], F32)
        scT = sb("scT", [128, 8], F32)
        cvt = sb("cvt", [128, 8], F32)
        gwb = sb("gwb", [128, 64], BF16)
        ones128 = sb("ones128", [128, 1], BF16)
        onescol = sb("onescol", [1, 128], BF16)
        i8 = sb("i8", [8, 8], F32)
        i128 = sb("i128", [128, 128], F32)
        selt = sb("selt", [8, 1024], BF16)
        gtok = sb("gtok", [128, 32], F32)
        msk = sb("msk", [128, 64], BF16)
        tops = sb("tops", [128, 32], F32)
        m_bf = sb("m_bf", [128, 32], F32)
        mT = sb("mT", [8, 512], BF16)
        m_bc = sb("m_bc", [128, 512], BF16)
        epsv = sb("epsv", [128, 1], F32)
        qtile = sb("qtile", [128, 8], F32)

        pb = [psf(f"pb{i}", [128, 512], F32) for i in range(8)]

        S = {}
        for name in ("smKT", "smV", "smW", "smWM", "smX", "smSP", "smSI",
                     "smX2", "smY", "smMISC", "smCV", "smXB", "smXC",
                     "sT", "sV", "sS"):
            S[name] = Sem(st.enter_context(nc.semaphore(name)), name)
        smKT, smV, smW, smWM, smX = (S[k] for k in ("smKT", "smV", "smW", "smWM", "smX"))
        smSP, smSI, smX2, smY, smMISC = (S[k] for k in ("smSP", "smSI", "smX2", "smY", "smMISC"))
        smCV = S["smCV"]
        smXB, smXC = S["smXB"], S["smXC"]
        sT, sV, sS = S["sT"], S["sV"], S["sS"]

        KTt = lambda dc, nch: wbig[:, dc * 8192 + nch * 128: dc * 8192 + nch * 128 + 128]
        Vt = lambda nch, dc: wbig[:, nch * 1024 + dc * 128: nch * 1024 + dc * 128 + 128]
        Wut = lambda e, dc, rc: wbig[:, (e * 8 + dc) * 256 + rc * 128:
                                     (e * 8 + dc) * 256 + rc * 128 + 128]
        Wvt = lambda e, dc, rc: wbig[:, 16384 + (e * 8 + dc) * 256 + rc * 128:
                                     16384 + (e * 8 + dc) * 256 + rc * 128 + 128]
        Wot = lambda e, rc, dc: wbig[:, 32768 + e * 2048 + rc * 1024 + dc * 128:
                                     32768 + e * 2048 + rc * 1024 + dc * 128 + 128]
        xh_chunk = lambda half, dc: xg[:, half * 4096 + dc * 512:
                                       half * 4096 + (dc + 1) * 512]
        hg_chunk = lambda dc: hg[:, dc * 512: (dc + 1) * 512]
        sslot = lambda e, rc: scb[:, (e * 2 + rc) * 512: (e * 2 + rc + 1) * 512]
        sustg = lambda i: su[:, (i % 2) * 512: (i % 2 + 1) * 512]
        sbstg = lambda i: sbS[:, (i % 2) * 512: (i % 2 + 1) * 512]
        S2s = lambda i: S2[:, (i % 2) * 512: (i % 2 + 1) * 512]

        rowF1 = rowlg[0:1, 0:512]
        rowF2 = rowlg[0:1, 512:1024]
        lgT = rowlg[0:8, 0:512]
        rowB1 = rowB[0:1, 0:512]
        rowB2 = rowB[0:1, 512:1024]
        wmslot = [xg[:, 4096: 6144], xg[:, 6144: 8192]]
        sa_c = lambda dc: modv[:, dc: dc + 1]
        ca1_c = lambda dc: modv[:, 8 + dc: 9 + dc]
        ga_c = lambda dc: modv[:, 16 + dc: 17 + dc]
        sb_c = lambda dc: modv[:, 24 + dc: 25 + dc]
        cb1_c = lambda dc: modv[:, 32 + dc: 33 + dc]
        gb_c = lambda dc: modv[:, 40 + dc: 41 + dc]

        xTr = xT.rearrange("(c p) t -> p c t", p=128)
        KTr = KT.rearrange("(c p) n -> p c n", p=128)
        Vr = V.rearrange("(c p) d -> p c d", p=128)
        yTr = yT.rearrange("(c p) t -> p c t", p=128)

        mmA = [[0] * NCH for _ in range(NG)]
        sqA = [[0] * NCH for _ in range(NG)]
        rawA = [[0] * NCH for _ in range(NG)]
        onesA = [[0] * NCH for _ in range(NG)]
        spillN = [0]
        hfin = [0] * NG
        h2fin = [0] * NG
        sumMM, sumsqMM, rowsdone, sqrtLN = {}, {}, {}, {}
        mubcMM, rsbcMM, mubcEv, rsbcEv = {}, {}, {}, {}
        cchainRecip = [0] * NG
        cchainSqrt = [0] * NG
        cbcMM = [0] * NG
        cbcEv = [0] * NG
        modmm = [0] * 24
        mmBc = [0] * (NG * NCH)
        oev = [[0] * 8 for _ in range(NG)]
        glogD = [0] * NG
        lgtEv = [0] * NG
        trlgD = [[0] * 4 for _ in range(NG)]
        trmD = [[0] * 4 for _ in range(NG)]
        mtEv = [[0] * 4 for _ in range(NG)]
        mbcMMn = [[0] * E for _ in range(NG)]
        mbcEv = [[0] * E for _ in range(NG)]
        siluD = [[[0] * 2 for _ in range(E)] for _ in range(NG)]
        sslD = [[[0] * 2 for _ in range(E)] for _ in range(NG)]
        ummD = [[[0] * 2 for _ in range(E)] for _ in range(NG)]
        vmmD = [[[0] * 2 for _ in range(E)] for _ in range(NG)]
        YmmD = [[0] * 8 for _ in range(NG)]
        YevD = [[0] * 8 for _ in range(NG)]
        mod_started = [False]
        mod1_done = [0]

        def phase_barrier():
            snap = [(sT, sT.n), (sV, sV.n), (sS, sS.n)]
            for eng in ENGINES:
                C.op(eng, None, waits=snap)

        C.op("vector", "memset", ones128[:], 1.0)
        C.op("vector", "memset", epsv[:], 1e-5)
        C.op("vector", "memset", qtile[:], 0.25)
        C.op("vector", "memset", onescol[:], 1.0, inc=sV)

        C.op("gpsimd", "dma_start", out=cvt[:],
             in_=cv.rearrange("(c p) -> p c", p=128),
             allow_slow_non_contiguous=True, inc=smCV, incn=16)
        for j in range(16):
            C.op("gpsimd", "dma_start",
                 out=wbig.rearrange("p (c n) -> p c n", c=8)[:, :, j * 512:(j + 1) * 512],
                 in_=KTr[:, :, j * 512:(j + 1) * 512], inc=smKT, incn=16)

        C.op("sync", "dma_start", out=i8[:], in_=idf, inc=smMISC, incn=16)
        C.op("sync", "dma_start", out=i128[:], in_=idb, inc=smMISC, incn=16)
        C.op("sync", "dma_start", out=selt[:], in_=selb, inc=smMISC, incn=16)

        C.op("scalar", "activation", scT[:], cvt[:], AF.Silu,
             waits=[(smCV, 16)], inc=sS)
        silu_c_done = sS.n

        wmodr = wmod.rearrange("(c p) n -> p c n", p=128)

        def emit_wm_load(j):
            C.op("sync", "dma_start",
                 out=wmslot[j % 2].rearrange("p (c n) -> p c n", c=8),
                 in_=wmodr[:, :, j * 256:(j + 1) * 256],
                 waits=[(sT, modmm[j - 2] if j >= 2 else 0)],
                 inc=smWM, incn=16)

        def emit_wm_mms(j):
            extra = [(sV, mod1_done[0])] if j == 12 else []
            for jl in range(2):
                col = j * 2 + jl
                for dc in range(8):
                    first = not mod_started[0]
                    mod_started[0] = True
                    C.op("tensor", "matmul", pb[6][:, col:col + 1],
                         wmslot[j % 2][:, dc * 256 + jl * 128:
                                       dc * 256 + (jl + 1) * 128],
                         scT[:, dc:dc + 1],
                         start=first, stop=(j == 23 and jl == 1 and dc == 7),
                         waits=[(smWM, 16 * (j + 1)), (sS, silu_c_done)] + extra,
                         inc=(sT if dc == 7 else None))
                modmm[j] = sT.n

        for j in range(12):
            emit_wm_load(j)
            emit_wm_mms(j)

        def emit_mod_evac(part):
            lo = 0 if part == 0 else 24
            gg = 16 if part == 0 else 40
            cc = 8 if part == 0 else 32
            mm = modmm[11 if part == 0 else 23]
            C.op("vector", "tensor_copy", modv[:, lo:lo + 8],
                 pb[6][:, lo:lo + 8], waits=[(sT, mm)])
            C.op("vector", "tensor_scalar", modv[:, cc:cc + 8],
                 pb[6][:, cc:cc + 8], 1.0, None, ALU.add)
            C.op("vector", "tensor_scalar", csc[:, 0:8],
                 pb[6][:, gg:gg + 8], 0.0, None, ALU.is_ge, inc=sV)
            C.op("scalar", "activation", csc[:, 8:16], pb[6][:, gg:gg + 8],
                 AF.Abs, waits=[(sT, mm), (sV, sV.n)], inc=sS)
            C.op("vector", "tensor_tensor", csc[:, 8:16], csc[:, 8:16],
                 qtile[:], ALU.max, waits=[(sS, sS.n)])
            C.op("vector", "tensor_tensor", modv[:, gg:gg + 8], csc[:, 0:8],
                 csc[:, 8:16], ALU.mult)
            C.op("vector", "scalar_tensor_tensor", modv[:, gg:gg + 8],
                 modv[:, gg:gg + 8], 2.0, csc[:, 8:16],
                 ALU.mult, ALU.subtract, inc=sV)
            if part == 0:
                mod1_done[0] = sV.n

        emit_mod_evac(0)

        def preA_thunks(g):
            th = []
            xh = lambda dc: xh_chunk(0, dc)
            xload = [0]

            def t_load():
                C.op("sync", "dma_start",
                     out=xg[:, 0:4096].rearrange("p (c t) -> p c t", c=8),
                     in_=xTr[:, :, g * TG:(g + 1) * TG],
                     waits=[(sV, hfin[g - 1] if g >= 1 else 0)],
                     inc=smX, incn=16)
                xload[0] = smX.n
            th.append(t_load)

            def mk_cast(dc):
                def f():
                    C.op("vector", "tensor_copy", sustg(dc), xh(dc),
                         waits=[(smX, xload[0]),
                                (sT, sumMM.get(("A", g, dc - 2), 0))],
                         inc=sV)
                    cn = sV.n
                    w = [(sV, cn)]
                    if dc == 0 and g >= 1:
                        w.append((sV, rowsdone[("A", g - 1)]))
                    C.op("tensor", "matmul", pb[2][0:1, 0:512], ones128[:],
                         sustg(dc), start=(dc == 0), stop=(dc == 7),
                         waits=w, inc=sT)
                    sumMM[("A", g, dc)] = sT.n
                return f
            for dc in range(8):
                th.append(mk_cast(dc))

            def mk_sq(dc):
                def f():
                    C.op("scalar", "activation", sustg(dc), xh(dc), AF.Square,
                         waits=[(sT, sumMM[("A", g, 7)] if dc < 2
                                 else sumsqMM[("A", g, dc - 2)])],
                         inc=sS)
                    sn = sS.n
                    w = [(sS, sn)]
                    if dc == 0 and g >= 1:
                        w.append((sV, rowsdone[("A", g - 1)]))
                    C.op("tensor", "matmul", pb[3][0:1, 0:512], ones128[:],
                         sustg(dc), start=(dc == 0), stop=(dc == 7),
                         waits=w, inc=sT)
                    sumsqMM[("A", g, dc)] = sT.n
                return f
            for dc in range(8):
                th.append(mk_sq(dc))

            def t_rows():
                C.op("vector", "tensor_scalar", rowB1, pb[2][0:1, 0:512],
                     1.0 / D, None, ALU.mult,
                     waits=[(sT, sumMM[("A", g, 7)]),
                            (sT, cbcMM[g - 2] if g >= 2 else 0)])
                C.op("vector", "tensor_scalar", rowF1, pb[3][0:1, 0:512],
                     1.0 / D, None, ALU.mult,
                     waits=[(sT, sumsqMM[("A", g, 7)]),
                            (sS, cchainSqrt[g - 2] if g >= 2 else 0)])
                C.op("vector", "tensor_tensor", rowF2, rowB1, rowB1, ALU.mult)
                C.op("vector", "tensor_tensor", rowF1, rowF1, rowF2,
                     ALU.subtract, inc=sV)
                C.op("scalar", "activation", rowF2, rowF1, AF.Sqrt, bias=epsv[0:1, 0:1],
                     waits=[(sV, sV.n)], inc=sS)
                sqrtLN[("A", g)] = sS.n
                C.op("vector", "reciprocal", rowF1, rowF2,
                     waits=[(sS, sS.n)])
                C.op("vector", "tensor_tensor", rowB2, rowF1, rowF1,
                     ALU.max, inc=sV)
                rowsdone[("A", g)] = sV.n
            th.append(t_rows)

            def t_bc1():
                C.op("tensor", "matmul", pb[5][:, 0:512], onescol[:], rowB1,
                     start=True, stop=True,
                     waits=[(sV, rowsdone[("A", g)]),
                            (sV, cbcEv[g - 1] if g >= 1 else 0)], inc=sT)
                mubcMM[("A", g)] = sT.n
                C.op("vector", "tensor_copy", su[:, 0:512], pb[5][:, 0:512],
                     waits=[(sT, sT.n), (sT, sumsqMM[("A", g, 7)])], inc=sV)
                mubcEv[("A", g)] = sV.n
            th.append(t_bc1)

            def t_bc2():
                C.op("tensor", "matmul", pb[5][:, 0:512], onescol[:], rowB2,
                     start=True, stop=True,
                     waits=[(sV, mubcEv[("A", g)])], inc=sT)
                rsbcMM[("A", g)] = sT.n
                C.op("vector", "tensor_copy", su[:, 512:1024], pb[5][:, 0:512],
                     waits=[(sT, sT.n)], inc=sV)
                rsbcEv[("A", g)] = sV.n
            th.append(t_bc2)

            def mk_norm(dc0):
                def f():
                    for dc in (dc0, dc0 + 1):
                        C.op("vector", "tensor_tensor", xh(dc), xh(dc),
                             su[:, 0:512], ALU.subtract,
                             waits=[(sV, rsbcEv[("A", g)])])
                        C.op("vector", "tensor_tensor", xh(dc), xh(dc),
                             su[:, 512:1024], ALU.mult,
                             inc=(sV if dc == 7 else None))
                return f
            for dc0 in (0, 2, 4, 6):
                th.append(mk_norm(dc0))
            return th

        def emit_hfinA(g):
            for dc in range(8):
                C.op("vector", "tensor_scalar", hg_chunk(dc), xh_chunk(0, dc),
                     ca1_c(dc), sa_c(dc), ALU.mult, ALU.add,
                     waits=[(sV, mod1_done[0]),
                            (sT, mmA[g - 1][NCH - 1] if g >= 1 else 0)],
                     inc=(sV if dc == 7 else None))
            hfin[g] = sV.n

        def emit_sweepA(g, hooks):
            hooks = list(hooks)
            for k in range(NCH):
                pbk = pb[k % 2]
                for dc in range(8):
                    w = []
                    if dc == 0:
                        w.append((sV, hfin[g]))
                        if g == 0:
                            w.append((smKT, 16 * (k // 4 + 1)))
                        if k >= 2:
                            w += [(sS, sqA[g][k - 2]), (sV, rawA[g][k - 2])]
                        elif g >= 1:
                            w += [(sS, sqA[g - 1][NCH - 2 + k]),
                                  (sV, rawA[g - 1][NCH - 2 + k])]
                    C.op("tensor", "matmul", pbk[:, 0:512], KTt(dc, k),
                         hg_chunk(dc), start=(dc == 0), stop=(dc == 7),
                         waits=w, inc=(sT if dc == 7 else None))
                mmA[g][k] = sT.n
                if k >= 1:
                    kk = k - 1
                    w = [(sS, sqA[g][kk])]
                    if kk == 0 and g >= 1:
                        w.append((sV, cchainRecip[g - 1]))
                    C.op("tensor", "matmul", pb[4][0:1, 0:512], ones128[:],
                         S2s(kk), start=(kk == 0), stop=False,
                         waits=w, inc=sT)
                    onesA[g][kk] = sT.n
                w = [(sT, mmA[g][k])]
                if k >= 2:
                    w.append((sT, onesA[g][k - 2]))
                elif g >= 1:
                    w.append((sT, onesA[g - 1][NCH - 2 + k]))
                C.op("scalar", "activation", S2s(k), pbk[:, 0:512], AF.Square,
                     waits=w, inc=sS)
                sqA[g][k] = sS.n
                u = k // SU
                w = [(sT, mmA[g][k]), (sS, sqA[g][k])]
                if k % SU == 0:
                    gu = g * NU + u
                    if gu >= 2:
                        w.append((smSP, 16 * (gu - 1)))
                C.op("vector", "tensor_copy",
                     sbS[:, (u % 2) * 2048 + (k % SU) * 512:
                         (u % 2) * 2048 + (k % SU + 1) * 512],
                     pbk[:, 0:512], waits=w, inc=sV)
                rawA[g][k] = sV.n
                if k % SU == SU - 1:
                    C.op("sync", "dma_start",
                         out=Sdram[:, g, u * SU:(u + 1) * SU, :],
                         in_=sbS[:, (u % 2) * 2048:(u % 2 + 1) * 2048].rearrange(
                             "p (c t) -> p c t", c=SU),
                         waits=[(sV, rawA[g][k])], inc=smSP, incn=16)
                    spillN[0] += 1
                if hooks and k >= 2:
                    hooks.pop(0)()
            kk = NCH - 1
            C.op("tensor", "matmul", pb[4][0:1, 0:512], ones128[:], S2s(kk),
                 start=False, stop=True, waits=[(sS, sqA[g][kk])], inc=sT)
            onesA[g][kk] = sT.n
            while hooks:
                hooks.pop(0)()
            C.op("vector", "reciprocal", rowF1, pb[4][0:1, 0:512],
                 waits=[(sT, onesA[g][kk]),
                        (sS, sqrtLN.get(("A", g + 1), 0))], inc=sV)
            cchainRecip[g] = sV.n
            C.op("scalar", "activation", rowB1, rowF1, AF.Sqrt,
                 scale=float(NB),
                 waits=[(sV, cchainRecip[g]),
                        (sT, mubcMM.get(("A", g + 1), 0))], inc=sS)
            cchainSqrt[g] = sS.n
            C.op("tensor", "matmul", pb[5][:, 0:512], onescol[:], rowB1,
                 start=True, stop=True,
                 waits=[(sS, cchainSqrt[g]),
                        (sV, rsbcEv.get(("A", g + 1), 0))], inc=sT)
            cbcMM[g] = sT.n
            C.op("vector", "tensor_copy", scb[:, g * 512:(g + 1) * 512],
                 pb[5][:, 0:512], waits=[(sT, cbcMM[g])], inc=sV)
            cbcEv[g] = sV.n

        for f in preA_thunks(0):
            f()
        emit_hfinA(0)
        if phases == "P":
            for eng in ENGINES:
                C.op(eng, None, waits=[(sV, hfin[0]), (sT, sT.n), (sS, sS.n)])
            if _DEBUG_IO:
                C.op("sync", "dma_start", out=dbgM, in_=modv[:], inc=smMISC, incn=16)
                C.op("sync", "dma_start", out=dbgH, in_=hg[:], inc=smMISC, incn=16)
                C.op("sync", None, waits=[(smMISC, smMISC.n)])
            return _replay(nc, C)
        if _DEBUG_IO:
            C.op("sync", "dma_start", out=dbgH,
                 in_=hg[:], waits=[(sV, hfin[0])], inc=smMISC, incn=16)
            C.op("sync", None, waits=[(smMISC, smMISC.n)])
        g0_hooks = []
        for j in range(12, 24):
            g0_hooks.append(lambda j=j: emit_wm_load(j))
            g0_hooks.append(lambda j=j: emit_wm_mms(j))
        g0_hooks.append(lambda: emit_mod_evac(1))
        g0_hooks += preA_thunks(1)
        emit_sweepA(0, g0_hooks)
        emit_hfinA(1)
        if phases == "A1":
            snap = [(sT, sT.n), (sV, sV.n), (sS, sS.n), (smSP, 16 * spillN[0])]
            for eng in ENGINES:
                C.op(eng, None, waits=snap)
            return _replay(nc, C)
        for g in range(1, NG):
            if g + 1 < NG:
                hooks = preA_thunks(g + 1)
            else:
                def bx0():
                    C.op("gpsimd", "dma_start",
                         out=xg[:, 0:4096].rearrange("p (c t) -> p c t", c=8),
                         in_=xTr[:, :, 0:TG],
                         waits=[(sV, hfin[NG - 1])], inc=smXB, incn=16)
                hooks = [bx0]
            emit_sweepA(g, hooks)
            if g + 1 < NG:
                emit_hfinA(g + 1)

        bx_load = smXB.n
        phase_barrier()

        if phases == "A":
            C.op("sync", None, waits=[(smSP, 16 * spillN[0])])
            if _DEBUG_IO:
                C.op("sync", "dma_start", out=dbgM, in_=modv[:],
                     waits=[(sV, sV.n), (sS, sS.n), (sT, sT.n)],
                     inc=smMISC, incn=16)
                C.op("sync", None, waits=[(smMISC, smMISC.n)])
            return _replay(nc, C)

        for vc in range(8):
            C.op("gpsimd", "dma_start",
                 out=wbig[:, vc * 8192:(vc + 1) * 8192].rearrange(
                     "p (c d) -> p c d", c=8),
                 in_=Vr[:, vc * 8:(vc + 1) * 8, :],
                 waits=[(sT, mmA[NG - 1][NCH - 1])], inc=smV, incn=16)

        bxof = [0] * NG
        bxof[0] = bx_load
        for g in range(NG):
            if g >= 1:
                C.op("gpsimd", "dma_start",
                     out=xg[:, 0:4096].rearrange("p (c t) -> p c t", c=8),
                     in_=xTr[:, :, g * TG:(g + 1) * TG],
                     waits=[(smX2, 16 * g)], inc=smXB, incn=16)
                bxof[g] = smXB.n
            for u in range(NU):
                gu = g * NU + u
                w = [(smSP, 16 * spillN[0])]
                if gu >= 2:
                    w.append((sT, mmBc[(gu - 2) * SU + SU - 1]))
                C.op("sync", "dma_start",
                     out=sbS[:, (u % 2) * 2048:(u % 2 + 1) * 2048].rearrange(
                         "p (c t) -> p c t", c=SU),
                     in_=Sdram[:, g, u * SU:(u + 1) * SU, :],
                     waits=w, inc=smSI, incn=16)
                si_n = smSI.n
                for cc in range(SU):
                    k = u * SU + cc
                    kg = g * NCH + k
                    sl = sbS[:, (u % 2) * 2048 + cc * 512:
                             (u % 2) * 2048 + (cc + 1) * 512]
                    C.op("vector", "tensor_tensor", sl, sl,
                         scb[:, g * 512:(g + 1) * 512], ALU.mult,
                         waits=[(smSI, si_n), (sV, cbcEv[g])], inc=sV)
                    C.op("scalar", "activation", sl, sl, AF.Gelu,
                         waits=[(sV, sV.n)], inc=sS)
                    gel_n = sS.n
                    for dc in range(8):
                        w = [(sS, gel_n)]
                        if k == 0 and g >= 1:
                            w.append((sV, oev[g - 1][dc]))
                        if g == 0:
                            w.append((smV, 16 * (k // 8 + 1)))
                        C.op("tensor", "matmul", pb[dc][:, 0:512], Vt(k, dc),
                             sl, start=(k == 0), stop=(k == NCH - 1),
                             waits=w, inc=(sT if dc == 7 else None))
                    mmBc[kg] = sT.n
            for dc in range(8):
                C.op("vector", "scalar_tensor_tensor", xh_chunk(0, dc),
                     pb[dc][:, 0:512], ga_c(dc), xh_chunk(0, dc),
                     ALU.mult, ALU.add,
                     waits=[(sT, mmBc[g * NCH + NCH - 1]), (smXB, bxof[g])],
                     inc=sV)
                oev[g][dc] = sV.n
            C.op("gpsimd", "dma_start", out=X2dram[:, g, :, :],
                 in_=xg[:, 0:4096].rearrange("p (c t) -> p c t", c=8),
                 waits=[(sV, oev[g][7])], inc=smX2, incn=16)

        b_last = mmBc[NG * NCH - 1]
        phase_barrier()

        if phases == "B":
            C.op("sync", None, waits=[(smX2, 16 * NG)])
            return _replay(nc, C)

        g7 = (NG - 1) * NCH
        C.op("gpsimd", "dma_start", out=gwb[:],
             in_=gw.rearrange("(c p) e -> p c e", p=128),
             allow_slow_non_contiguous=True,
             waits=[(sT, mmBc[g7 + 15])], inc=smW, incn=16)
        for e in range(E):
            C.op("gpsimd", "dma_start",
                 out=wbig[:, e * 2048:(e + 1) * 2048].rearrange(
                     "p (c r) -> p c r", c=8),
                 in_=Wu[e].rearrange("(c p) r -> p c r", p=128),
                 waits=[(sT, mmBc[g7 + 15])], inc=smW, incn=16)
        for e in range(E):
            C.op("gpsimd", "dma_start",
                 out=wbig[:, 16384 + e * 2048: 16384 + (e + 1) * 2048].rearrange(
                     "p (c r) -> p c r", c=8),
                 in_=Wv[e].rearrange("(c p) r -> p c r", p=128),
                 waits=[(sT, mmBc[g7 + 31])], inc=smW, incn=16)
        for e in range(E):
            C.op("gpsimd", "dma_start",
                 out=wbig[:, 32768 + e * 2048: 32768 + (e + 1) * 2048].rearrange(
                     "p (c d) -> p c d", c=2),
                 in_=Wo[e].rearrange("(c p) d -> p c d", p=128),
                 waits=[(sT, mmBc[g7 + 47])], inc=smW, incn=16)
        smW_all = smW.n

        def preC_thunks(g):
            th = []
            half = (g + 1) % 2
            xh = lambda dc: xh_chunk(half, dc)
            xload = [0]

            def t_load():
                w = [(smX2, 16 * (g + 1))]
                if g >= 2:
                    w.append((smY, 16 * (g - 1)))
                C.op("gpsimd", "dma_start",
                     out=xg[:, half * 4096: half * 4096 + 4096].rearrange(
                         "p (c t) -> p c t", c=8),
                     in_=X2dram[:, g, :, :], waits=w, inc=smXC, incn=16)
                xload[0] = smXC.n
            th.append(t_load)

            def mk_cast(dc):
                def f():
                    w = [(smXC, xload[0]),
                         (sT, sumMM.get(("C", g, dc - 2), 0))]
                    if dc < 2 and g >= 1:
                        w.append((sT, sumsqMM[("C", g - 1, 6 + dc)]))
                    C.op("vector", "tensor_copy", sbstg(dc), xh(dc),
                         waits=w, inc=sV)
                    cn = sV.n
                    w = [(sV, cn)]
                    if dc == 0 and g >= 1:
                        w += [(sV, rowsdone[("C", g - 1)]),
                              (sV, YevD[g - 1][6])]
                    C.op("tensor", "matmul", pb[6][0:1, 0:512], ones128[:],
                         sbstg(dc), start=(dc == 0), stop=(dc == 7),
                         waits=w, inc=sT)
                    sumMM[("C", g, dc)] = sT.n
                return f
            for dc in range(8):
                th.append(mk_cast(dc))

            def mk_sq(dc):
                def f():
                    C.op("scalar", "activation", sbstg(dc), xh(dc), AF.Square,
                         waits=[(sT, sumMM[("C", g, 7)] if dc < 2
                                 else sumsqMM[("C", g, dc - 2)])],
                         inc=sS)
                    sn = sS.n
                    w = [(sS, sn)]
                    if dc == 0 and g >= 1:
                        w.append((sV, YevD[g - 1][7]))
                    C.op("tensor", "matmul", pb[7][0:1, 0:512], ones128[:],
                         sbstg(dc), start=(dc == 0), stop=(dc == 7),
                         waits=w, inc=sT)
                    sumsqMM[("C", g, dc)] = sT.n
                return f
            for dc in range(8):
                th.append(mk_sq(dc))

            def t_rows():
                C.op("vector", "tensor_scalar", rowB1, pb[6][0:1, 0:512],
                     1.0 / D, None, ALU.mult,
                     waits=[(sT, sumMM[("C", g, 7)]),
                            (sT, mubcMM.get(("C", g - 1), 0))])
                C.op("vector", "tensor_scalar", rowF1, pb[7][0:1, 0:512],
                     1.0 / D, None, ALU.mult,
                     waits=[(sT, sumsqMM[("C", g, 7)]),
                            (sT, trlgD[g - 1][3] if g >= 1 else 0),
                            (sS, sqrtLN.get(("C", g - 1), 0))])
                C.op("vector", "tensor_tensor", rowF2, rowB1, rowB1, ALU.mult)
                C.op("vector", "tensor_tensor", rowF1, rowF1, rowF2,
                     ALU.subtract, inc=sV)
                C.op("scalar", "activation", rowF2, rowF1, AF.Sqrt, bias=epsv[0:1, 0:1],
                     waits=[(sV, sV.n)], inc=sS)
                sqrtLN[("C", g)] = sS.n
                C.op("vector", "reciprocal", rowF1, rowF2,
                     waits=[(sS, sS.n)])
                C.op("vector", "tensor_tensor", rowB2, rowF1, rowF1,
                     ALU.max, inc=sV)
                rowsdone[("C", g)] = sV.n
            th.append(t_rows)

            def t_bc1():
                C.op("tensor", "matmul", pb[4][:, 0:512], onescol[:], rowB1,
                     start=True, stop=True,
                     waits=[(sV, rowsdone[("C", g)]),
                            (sV, mtEv[g - 1][3] if g >= 1 else 0)], inc=sT)
                mubcMM[("C", g)] = sT.n
                C.op("vector", "tensor_copy", sbS[:, 1024:1536],
                     pb[4][:, 0:512], waits=[(sT, sT.n)], inc=sV)
                mubcEv[("C", g)] = sV.n
            th.append(t_bc1)

            def t_bc2():
                C.op("tensor", "matmul", pb[4][:, 0:512], onescol[:], rowB2,
                     start=True, stop=True,
                     waits=[(sV, mubcEv[("C", g)])], inc=sT)
                rsbcMM[("C", g)] = sT.n
                C.op("vector", "tensor_copy", sbS[:, 1536:2048],
                     pb[4][:, 0:512], waits=[(sT, sT.n)], inc=sV)
                rsbcEv[("C", g)] = sV.n
            th.append(t_bc2)

            return th

        def emit_h2fin(g):
            half = (g + 1) % 2
            for dc in range(8):
                w = [(sV, mod1_done[0]), (sV, rsbcEv[("C", g)])]
                if g >= 1:
                    w.append((sT, vmmD[g - 1][E - 1][1]))
                C.op("vector", "tensor_tensor", hg_chunk(dc),
                     xh_chunk(half, dc), sbS[:, 1024:1536], ALU.subtract,
                     waits=w)
                C.op("vector", "tensor_tensor", hg_chunk(dc), hg_chunk(dc),
                     sbS[:, 1536:2048], ALU.mult)
                C.op("vector", "tensor_scalar", hg_chunk(dc), hg_chunk(dc),
                     cb1_c(dc), sb_c(dc), ALU.mult, ALU.add,
                     inc=(sV if dc == 7 else None))
            h2fin[g] = sV.n

        def emit_groupC(g, hooks):
            hooks = list(hooks)
            half = (g + 1) % 2
            for dc in range(8):
                w = [(sV, h2fin[g]), (smW, 16)]
                if dc == 0 and g >= 1:
                    w.append((sS, siluD[g - 1][E - 1][0]))
                C.op("tensor", "matmul", pb[0][0:8, 0:512],
                     gwb[:, dc * 8:(dc + 1) * 8], hg_chunk(dc),
                     start=(dc == 0), stop=(dc == 7),
                     waits=w, inc=(sT if dc == 7 else None))
            glogD[g] = sT.n
            C.op("vector", "tensor_copy", lgT, pb[0][0:8, 0:512],
                 waits=[(sT, glogD[g]), (sS, sqrtLN[("C", g)]),
                        (sT, trlgD[g - 1][3] if g >= 1 else 0)], inc=sV)
            lgtEv[g] = sV.n
            gcpy = 0
            for sub in range(4):
                w = [(sV, lgtEv[g]), (smMISC, 48)]
                if sub >= 1:
                    w.append((sV, gcpy))
                elif g >= 1:
                    w.append((sV, mtEv[g - 1][3]))
                C.op("tensor", "transpose", pb[4][0:128, 0:8],
                     lgT[0:8, sub * 128:(sub + 1) * 128], i8[:],
                     waits=w, inc=sT)
                trlgD[g][sub] = sT.n
                C.op("vector", "tensor_copy", gtok[:, sub * 8:(sub + 1) * 8],
                     pb[4][:, 0:8],
                     waits=[(sT, trlgD[g][sub]),
                            (sT, trmD[g - 1][3] if (sub == 0 and g >= 1) else 0)],
                     inc=sV)
                gcpy = sV.n
            for rc in range(2):
                for dc in range(8):
                    w = [(smW, 16 * 2)]
                    if dc == 0:
                        if rc == 0:
                            w.append((sV, lgtEv[g]))
                            if g >= 1:
                                w.append((sS, siluD[g - 1][E - 1][0]))
                        elif g >= 1:
                            w.append((sS, siluD[g - 1][E - 1][1]))
                    C.op("tensor", "matmul", pb[rc][:, 0:512],
                         Wut(0, dc, rc), hg_chunk(dc),
                         start=(dc == 0), stop=(dc == 7),
                         waits=w, inc=(sT if dc == 7 else None))
                ummD[g][0][rc] = sT.n
                for dc in range(8):
                    w = [(smW, 16 * 10)]
                    if dc == 0 and g >= 1:
                        w.append((sV, sslD[g - 1][E - 1][rc]))
                    C.op("tensor", "matmul", pb[2 + rc][:, 0:512],
                         Wvt(0, dc, rc), hg_chunk(dc),
                         start=(dc == 0), stop=(dc == 7),
                         waits=w, inc=(sT if dc == 7 else None))
                vmmD[g][0][rc] = sT.n
            gts = lambda s: gtok[:, s * 8:(s + 1) * 8]
            tcs = gts
            mk0 = lambda s: msk[:, s * 8:(s + 1) * 8]
            mk1 = lambda s: msk[:, 32 + s * 8:32 + (s + 1) * 8]
            tp = lambda s, i: tops[:, s * 8 + i:s * 8 + i + 1]
            mbs = lambda s: m_bf[:, s * 8:(s + 1) * 8]
            for s in range(4):
                C.op("vector", "reduce_max", tp(s, 0), gts(s), axis=AX.X)
            for s in range(4):
                C.op("vector", "tensor_scalar", mk0(s), gts(s), tp(s, 0),
                     None, ALU.is_equal)
            for s in range(4):
                C.op("vector", "scalar_tensor_tensor", tcs(s), mk0(s), -1e30,
                     gts(s), ALU.mult, ALU.add)
            for s in range(4):
                C.op("vector", "reduce_max", tp(s, 1), tcs(s), axis=AX.X)
            for s in range(4):
                C.op("vector", "tensor_scalar", mk1(s), tcs(s), tp(s, 1),
                     None, ALU.is_equal)
            for s in range(4):
                C.op("vector", "tensor_tensor", tp(s, 2), tp(s, 0), tp(s, 1),
                     ALU.subtract, inc=(sV if s == 3 else None))
            for s in range(4):
                C.op("scalar", "activation", tp(s, 3), tp(s, 2), AF.Sigmoid,
                     waits=[(sV, sV.n)], inc=(sS if s == 3 else None))
            for s in range(4):
                C.op("scalar", "activation", tp(s, 4), tp(s, 2), AF.Sigmoid,
                     scale=-1.0, inc=(sS if s == 3 else None))
            for s in range(4):
                C.op("vector", "tensor_scalar", mbs(s), mk0(s), tp(s, 3),
                     None, ALU.mult, waits=[(sS, sS.n)])
            for s in range(4):
                C.op("vector", "scalar_tensor_tensor", mbs(s), mk1(s),
                     tp(s, 4), mbs(s), ALU.mult, ALU.add,
                     inc=(sV if s == 3 else None))
            mdone = sV.n
            for sub in range(4):
                w = [(sV, mdone)]
                if sub >= 1:
                    w.append((sV, mtEv[g][sub - 1]))
                C.op("tensor", "transpose", pb[4][0:8, 0:128], mbs(sub),
                     i128[:], waits=w, inc=sT)
                trmD[g][sub] = sT.n
                w = [(sT, trmD[g][sub])]
                if sub == 0 and g >= 1:
                    w.append((sT, mbcMMn[g - 1][E - 1]))
                C.op("vector", "tensor_copy",
                     mT[0:8, sub * 128:(sub + 1) * 128], pb[4][0:8, 0:128],
                     waits=w, inc=sV)
                mtEv[g][sub] = sV.n
            for e in range(E):
                w = [(sV, mtEv[g][3]), (smMISC, 48)]
                if e >= 1:
                    w.append((sV, mbcEv[g][e - 1]))
                elif g >= 1:
                    w.append((sV, mbcEv[g - 1][E - 1]))
                C.op("tensor", "matmul", pb[5][:, 0:512],
                     selt[0:8, e * 128:(e + 1) * 128], mT[0:8, 0:512],
                     start=True, stop=True, waits=w, inc=sT)
                mbcMMn[g][e] = sT.n
                C.op("vector", "tensor_copy", m_bc[:], pb[5][:, 0:512],
                     waits=[(sT, mbcMMn[g][e]),
                            (sV, sslD[g][e - 1][1] if e >= 1 else
                             (sslD[g - 1][E - 1][1] if g >= 1 else 0))],
                     inc=sV)
                mbcEv[g][e] = sV.n
                for rc in range(2):
                    if e >= 1:
                        for dc in range(8):
                            w = [(smW, 16 * (2 + e))]
                            if dc == 0:
                                w.append((sS, siluD[g][e - 1][rc]))
                            C.op("tensor", "matmul", pb[rc][:, 0:512],
                                 Wut(e, dc, rc), hg_chunk(dc),
                                 start=(dc == 0), stop=(dc == 7),
                                 waits=w, inc=(sT if dc == 7 else None))
                        ummD[g][e][rc] = sT.n
                        for dc in range(8):
                            w = [(smW, 16 * (10 + e))]
                            if dc == 0:
                                w.append((sV, sslD[g][e - 1][rc]))
                            C.op("tensor", "matmul", pb[2 + rc][:, 0:512],
                                 Wvt(e, dc, rc), hg_chunk(dc),
                                 start=(dc == 0), stop=(dc == 7),
                                 waits=w, inc=(sT if dc == 7 else None))
                        vmmD[g][e][rc] = sT.n
                    w = [(sT, ummD[g][e][rc])]
                    if e >= 1:
                        w.append((sV, sslD[g][e - 1][rc]))
                    elif g >= 1:
                        w.append((sV, sslD[g - 1][E - 1][rc]))
                    C.op("scalar", "activation", sustg(rc), pb[rc][:, 0:512],
                         AF.Silu, waits=w, inc=sS)
                    siluD[g][e][rc] = sS.n
                    C.op("vector", "tensor_tensor", sslot(e, rc), sustg(rc),
                         m_bc[:], ALU.mult,
                         waits=[(sS, siluD[g][e][rc]),
                                (sV, mbcEv[g][e]),
                                (sT, YmmD[g - 1][7] if g >= 1 else 0)])
                    C.op("vector", "tensor_tensor", sslot(e, rc),
                         sslot(e, rc), pb[2 + rc][:, 0:512], ALU.mult,
                         waits=[(sT, vmmD[g][e][rc])], inc=sV)
                    sslD[g][e][rc] = sV.n
                for _ in range(3):
                    if hooks:
                        hooks.pop(0)()
            while hooks:
                hooks.pop(0)()
            if g + 1 < NG and ("C", g + 1) in rsbcEv:
                emit_h2fin(g + 1)
            for dc in range(8):
                pby = pb[6 + dc % 2]
                for i, (e, rc) in enumerate((e, rc) for e in range(E)
                                            for rc in range(2)):
                    w = []
                    if i == 0:
                        w = [(sV, sslD[g][E - 1][1]), (smW, 16 * 25)]
                        if dc >= 2:
                            w.append((sV, YevD[g][dc - 2]))
                        elif g + 1 < NG:
                            w.append((sV, rowsdone.get(("C", g + 1), 0)))
                    C.op("tensor", "matmul", pby[:, 0:512], Wot(e, rc, dc),
                         sslot(e, rc), start=(i == 0), stop=(i == 15),
                         waits=w, inc=(sT if i == 15 else None))
                YmmD[g][dc] = sT.n
                C.op("vector", "scalar_tensor_tensor", xh_chunk(half, dc),
                     pby[:, 0:512], gb_c(dc), xh_chunk(half, dc),
                     ALU.mult, ALU.add, waits=[(sT, YmmD[g][dc])], inc=sV)
                YevD[g][dc] = sV.n
            C.op("sync", "dma_start", out=yTr[:, :, g * TG:(g + 1) * TG],
                 in_=xg[:, half * 4096: half * 4096 + 4096].rearrange(
                     "p (c t) -> p c t", c=8),
                 waits=[(sV, YevD[g][7])], inc=smY, incn=16)

        for f in preC_thunks(0):
            f()
        emit_h2fin(0)
        if phases == "C1":
            emit_groupC(0, [])
            snap = [(sT, sT.n), (sV, sV.n), (sS, sS.n)]
            for eng in ENGINES:
                C.op(eng, None, waits=snap)
            if _DEBUG_IO:
                C.op("sync", "dma_start", out=dbgH, in_=hg[:], inc=smMISC, incn=16)
                C.op("sync", "dma_start", out=dbgC, in_=scb[:], inc=smMISC, incn=16)
                C.op("sync", "dma_start", out=dbgT, in_=mT[:], inc=smMISC, incn=16)
                C.op("sync", "dma_start", out=dbgB, in_=m_bc[:], inc=smMISC, incn=16)
                C.op("sync", "dma_start", out=dbgS, in_=sbS[:], inc=smMISC, incn=16)
                C.op("sync", "dma_start", out=dbgX, in_=xg[:], inc=smMISC, incn=16)
                C.op("sync", "dma_start", out=dbgM, in_=modv[:], inc=smMISC, incn=16)
                C.op("sync", None, waits=[(smMISC, smMISC.n)])
            C.op("sync", None, waits=[(smY, 16 * 1)])
            return _replay(nc, C)
        for g in range(NG):
            hooks = preC_thunks(g + 1) if g + 1 < NG else []
            emit_groupC(g, hooks)

        if _DEBUG_IO:
            C.op("sync", "dma_start", out=dbgM, in_=modv[:],
                 waits=[(sV, sV.n), (sS, sS.n), (sT, sT.n)], inc=smMISC, incn=16)
            C.op("sync", "dma_start", out=dbgH, in_=hg[:], inc=smMISC, incn=16)
            C.op("sync", "dma_start", out=dbgT, in_=mT[:], inc=smMISC, incn=16)
            C.op("sync", "dma_start", out=dbgC, in_=scb[:],
                 waits=[(smMISC, smMISC.n), (sV, sV.n)], inc=smMISC, incn=16)
            C.op("sync", None, waits=[(smMISC, smMISC.n)])
        C.op("sync", None, waits=[(smY, 16 * NG)])
        return _replay(nc, C)


def _replay(nc, C):
    with nc.Block() as block:
        @block.tensor
        def _(eng):
            for f in C.prog["tensor"]:
                f(eng)

        @block.vector
        def _(eng):
            for f in C.prog["vector"]:
                f(eng)

        @block.scalar
        def _(eng):
            for f in C.prog["scalar"]:
                f(eng)

        @block.sync
        def _(eng):
            for f in C.prog["sync"]:
                f(eng)

        @block.gpsimd
        def _(eng):
            for f in C.prog["gpsimd"]:
                f(eng)
    return nc


_NC_CACHE = []


def kernel(L0_attn_q, L0_attn_k, L0_attn_v, L0_ffn_gate_w, L0_ffn_Wu,
           L0_ffn_Wv, L0_ffn_Wo, L0_c, L0_wmod):
    x = np.asarray(L0_attn_q, dtype=np.float32)
    K = np.asarray(L0_attn_k, dtype=np.float32)
    Vv = np.ascontiguousarray(np.asarray(L0_attn_v, dtype=np.float32))
    gwa = np.ascontiguousarray(np.asarray(L0_ffn_gate_w, dtype=np.float32))
    Wua = np.ascontiguousarray(np.asarray(L0_ffn_Wu, dtype=np.float32))
    Wva = np.ascontiguousarray(np.asarray(L0_ffn_Wv, dtype=np.float32))
    Woa = np.ascontiguousarray(np.asarray(L0_ffn_Wo, dtype=np.float32))
    Ca = np.asarray(L0_c, dtype=np.float32)
    wmoda = np.ascontiguousarray(np.asarray(L0_wmod, dtype=np.float32))
    KTa = np.ascontiguousarray(K.T)
    idf = np.eye(8, dtype=np.float32)
    idb = np.eye(128, dtype=np.float32)
    sel = np.zeros((8, 1024), dtype=ml_dtypes.bfloat16)
    for e in range(8):
        sel[e, e * 128:(e + 1) * 128] = 1

    if not _NC_CACHE:
        _NC_CACHE.append(build_nc())
    nc = _NC_CACHE[0]

    in_maps = []
    for b in range(8):
        in_maps.append({
            "xT": np.ascontiguousarray(x[b].T),
            "KT": KTa, "V": Vv, "Wu": Wua, "Wv": Wva, "Wo": Woa,
            "gw": gwa, "wmod": wmoda, "cv": np.ascontiguousarray(Ca[b]),
            "idf": idf, "idb": idb, "selb": sel,
        })
    res = run_bass_kernel_spmd(nc, in_maps, core_ids=list(range(8)))
    out = np.empty((8, 4096, 1024), np.float32)
    for b in range(8):
        out[b] = np.asarray(res.results[b]["yT"]).T
    return out


# version 04 (final)
# speedup vs baseline: 1.0316x; 1.0316x over previous
import numpy as np
import ml_dtypes
import concourse.bass as bass
import concourse.mybir as mybir
from concourse.bass_utils import run_bass_kernel_spmd
from contextlib import ExitStack

F32 = mybir.dt.float32
BF16 = mybir.dt.bfloat16
AF = mybir.ActivationFunctionType
ALU = mybir.AluOpType
AX = mybir.AxisListType

D = 1024
NB = 8192
T = 4096
TG = 512
NG = T // TG
NCH = NB // 128
DCH = D // 128
E = 8
SU = 4
NU = NCH // SU

ENGINES = ("tensor", "vector", "scalar", "sync", "gpsimd")


class Sem:
    def __init__(self, handle, name):
        self.h = handle
        self.name = name
        self.n = 0


_HAZ_READERS = {"tensor_copy", "tensor_scalar", "tensor_scalar_add",
                "tensor_scalar_sub", "tensor_scalar_mul", "tensor_scalar_max",
                "tensor_scalar_min", "reciprocal", "reduce_max", "reduce_sum",
                "tensor_reduce"}


def _ap_name(a):
    try:
        return a.tensor.name
    except AttributeError:
        return None


class Ctx:
    def __init__(self):
        self.prog = {e: [] for e in ENGINES}
        self.mark = {e: {} for e in ENGINES}
        self._last_vec_write = None

    def op(self, eng, method, *args, waits=(), inc=None, incn=1, **kw):
        if eng == "vector" and method is not None:
            if self._last_vec_write is not None:
                names = {_ap_name(a) for a in args} - {None}
                for s1, s2 in (kw.get("waits") or ()):
                    pass
                if self._last_vec_write in names:
                    self.prog["vector"].append(lambda E: E.engine_nop())
            self._last_vec_write = _ap_name(args[0]) if args else None
        wl = []
        for s, v in waits:
            if v <= 0:
                continue
            assert v <= s.n, f"wait on future {s.name}: {v} > {s.n} ({eng} {method})"
            if self.mark[eng].get(s.name, 0) < v:
                wl.append((s.h, v))
                self.mark[eng][s.name] = v
        if inc is not None:
            inc.n += incn

        def run(Eng, method=method, args=args, kw=kw, wl=wl, inc=inc, incn=incn):
            for h, v in wl:
                Eng.wait_ge(h, v)
            if method is None:
                return
            r = getattr(Eng, method)(*args, **kw)
            if inc is not None:
                r.then_inc(inc.h, incn)

        self.prog[eng].append(run)


_PHASES = "ABC"
_DEBUG_IO = False


def build_nc(phases=None):
    phases = phases or _PHASES
    nc = bass.Bass("TRN2", target_bir_lowering=False)

    xT = nc.dram_tensor("xT", [D, T], F32, kind="ExternalInput").ap()
    KT = nc.dram_tensor("KT", [D, NB], F32, kind="ExternalInput").ap()
    V = nc.dram_tensor("V", [NB, D], F32, kind="ExternalInput").ap()
    Wu = nc.dram_tensor("Wu", [E, D, 256], F32, kind="ExternalInput").ap()
    Wv = nc.dram_tensor("Wv", [E, D, 256], F32, kind="ExternalInput").ap()
    Wo = nc.dram_tensor("Wo", [E, 256, D], F32, kind="ExternalInput").ap()
    gw = nc.dram_tensor("gw", [D, E], F32, kind="ExternalInput").ap()
    wmod = nc.dram_tensor("wmod", [D, 6 * D], F32, kind="ExternalInput").ap()
    cv = nc.dram_tensor("cv", [D], F32, kind="ExternalInput").ap()
    idf = nc.dram_tensor("idf", [8, 8], F32, kind="ExternalInput").ap()
    idb = nc.dram_tensor("idb", [128, 128], F32, kind="ExternalInput").ap()
    selb = nc.dram_tensor("selb", [8, 1024], BF16, kind="ExternalInput").ap()

    yT = nc.dram_tensor("yT", [D, T], F32, kind="ExternalOutput").ap()
    skind = "ExternalOutput" if _DEBUG_IO else "Internal"
    Sdram = nc.dram_tensor("Sdram", [128, NG, NCH, TG], BF16, kind=skind).ap()
    X2dram = nc.dram_tensor("X2dram", [128, NG, DCH, TG], F32, kind=skind).ap()
    if _DEBUG_IO:
        dbgM = nc.dram_tensor("dbgM", [128, 48], F32, kind="ExternalOutput").ap()
        dbgH = nc.dram_tensor("dbgH", [128, 4096], BF16, kind="ExternalOutput").ap()
        dbgC = nc.dram_tensor("dbgC", [128, 8192], BF16, kind="ExternalOutput").ap()
        dbgT = nc.dram_tensor("dbgT", [8, 512], BF16, kind="ExternalOutput").ap()
        dbgB = nc.dram_tensor("dbgB", [128, 512], BF16, kind="ExternalOutput").ap()
        dbgS = nc.dram_tensor("dbgS", [128, 4096], BF16, kind="ExternalOutput").ap()
        dbgX = nc.dram_tensor("dbgX", [128, 8192], F32, kind="ExternalOutput").ap()

    C = Ctx()

    with ExitStack() as st:
        sb = lambda *a: st.enter_context(nc.sbuf_tensor(*a))
        psf = lambda *a: st.enter_context(nc.psum_tensor(*a))

        wbig = sb("wbig", [128, 65536], BF16)
        xg = sb("xg", [128, 8192], F32)
        hg = sb("hg", [128, 4096], BF16)
        sbS = sb("sbS", [128, 4096], BF16)
        su = sb("su", [128, 1024], BF16)
        S2 = sb("S2", [128, 1024], BF16)
        scb = sb("scb", [128, 8192], BF16)
        rowlg = sb("rowlg", [8, 1024], F32)
        rowB = sb("rowB", [1, 1024], BF16)
        modv = sb("modv", [128, 48], F32)
        csc = sb("csc", [128, 16], F32)
        scT = sb("scT", [128, 8], F32)
        cvt = sb("cvt", [128, 8], F32)
        gwb = sb("gwb", [128, 64], BF16)
        ones128 = sb("ones128", [128, 1], BF16)
        onescol = sb("onescol", [1, 128], BF16)
        i8 = sb("i8", [8, 8], F32)
        i128 = sb("i128", [128, 128], F32)
        selt = sb("selt", [8, 1024], BF16)
        gtok = sb("gtok", [128, 32], F32)
        msk = sb("msk", [128, 64], BF16)
        tops = sb("tops", [128, 32], F32)
        m_bf = sb("m_bf", [128, 32], F32)
        mT = sb("mT", [8, 512], BF16)
        m_bc = sb("m_bc", [128, 512], BF16)
        epsv = sb("epsv", [128, 1], F32)
        qtile = sb("qtile", [128, 8], F32)

        pb = [psf(f"pb{i}", [128, 512], F32) for i in range(8)]

        S = {}
        for name in ("smKT", "smV", "smW", "smWM", "smX", "smSP", "smSI",
                     "smX2", "smY", "smMISC", "smCV", "smXB", "smXC",
                     "sT", "sV", "sS"):
            S[name] = Sem(st.enter_context(nc.semaphore(name)), name)
        smKT, smV, smW, smWM, smX = (S[k] for k in ("smKT", "smV", "smW", "smWM", "smX"))
        smSP, smSI, smX2, smY, smMISC = (S[k] for k in ("smSP", "smSI", "smX2", "smY", "smMISC"))
        smCV = S["smCV"]
        smXB, smXC = S["smXB"], S["smXC"]
        sT, sV, sS = S["sT"], S["sV"], S["sS"]

        KTt = lambda dc, nch: wbig[:, dc * 8192 + nch * 128: dc * 8192 + nch * 128 + 128]
        Vt = lambda nch, dc: wbig[:, nch * 1024 + dc * 128: nch * 1024 + dc * 128 + 128]
        Wut = lambda e, dc, rc: wbig[:, (e * 8 + dc) * 256 + rc * 128:
                                     (e * 8 + dc) * 256 + rc * 128 + 128]
        Wvt = lambda e, dc, rc: wbig[:, 16384 + (e * 8 + dc) * 256 + rc * 128:
                                     16384 + (e * 8 + dc) * 256 + rc * 128 + 128]
        Wot = lambda e, rc, dc: wbig[:, 32768 + e * 2048 + rc * 1024 + dc * 128:
                                     32768 + e * 2048 + rc * 1024 + dc * 128 + 128]
        xh_chunk = lambda half, dc: xg[:, half * 4096 + dc * 512:
                                       half * 4096 + (dc + 1) * 512]
        hg_chunk = lambda dc: hg[:, dc * 512: (dc + 1) * 512]
        sslot = lambda e, rc: scb[:, (e * 2 + rc) * 512: (e * 2 + rc + 1) * 512]
        sustg = lambda i: su[:, (i % 2) * 512: (i % 2 + 1) * 512]
        sbstg = lambda i: sbS[:, (i % 2) * 512: (i % 2 + 1) * 512]
        S2s = lambda i: S2[:, (i % 2) * 512: (i % 2 + 1) * 512]

        rowF1 = rowlg[0:1, 0:512]
        rowF2 = rowlg[0:1, 512:1024]
        lgT = rowlg[0:8, 0:512]
        rowB1 = rowB[0:1, 0:512]
        rowB2 = rowB[0:1, 512:1024]
        wmslot = [xg[:, 4096: 6144], xg[:, 6144: 8192]]
        sa_c = lambda dc: modv[:, dc: dc + 1]
        ca1_c = lambda dc: modv[:, 8 + dc: 9 + dc]
        ga_c = lambda dc: modv[:, 16 + dc: 17 + dc]
        sb_c = lambda dc: modv[:, 24 + dc: 25 + dc]
        cb1_c = lambda dc: modv[:, 32 + dc: 33 + dc]
        gb_c = lambda dc: modv[:, 40 + dc: 41 + dc]

        xTr = xT.rearrange("(c p) t -> p c t", p=128)
        KTr = KT.rearrange("(c p) n -> p c n", p=128)
        Vr = V.rearrange("(c p) d -> p c d", p=128)
        yTr = yT.rearrange("(c p) t -> p c t", p=128)

        mmA = [[0] * NCH for _ in range(NG)]
        sqA = [[0] * NCH for _ in range(NG)]
        rawA = [[0] * NCH for _ in range(NG)]
        onesA = [[0] * NCH for _ in range(NG)]
        spillN = [0]
        hfin = [0] * NG
        h2fin = [0] * NG
        sumMM, sumsqMM, rowsdone, sqrtLN = {}, {}, {}, {}
        mubcMM, rsbcMM, mubcEv, rsbcEv = {}, {}, {}, {}
        cchainRecip = [0] * NG
        cchainSqrt = [0] * NG
        cbcMM = [0] * NG
        cbcEv = [0] * NG
        modmm = [0] * 24
        mmBc = [0] * (NG * NCH)
        oev = [[0] * 8 for _ in range(NG)]
        glogD = [0] * NG
        lgtEv = [0] * NG
        trlgD = [[0] * 4 for _ in range(NG)]
        trmD = [[0] * 4 for _ in range(NG)]
        mtEv = [[0] * 4 for _ in range(NG)]
        mbcMMn = [[0] * E for _ in range(NG)]
        mbcEv = [[0] * E for _ in range(NG)]
        siluD = [[[0] * 2 for _ in range(E)] for _ in range(NG)]
        sslD = [[[0] * 2 for _ in range(E)] for _ in range(NG)]
        ummD = [[[0] * 2 for _ in range(E)] for _ in range(NG)]
        vmmD = [[[0] * 2 for _ in range(E)] for _ in range(NG)]
        YmmD = [[0] * 8 for _ in range(NG)]
        YevD = [[0] * 8 for _ in range(NG)]
        mod_started = [False]
        mod1_done = [0]

        def phase_barrier():
            snap = [(sT, sT.n), (sV, sV.n), (sS, sS.n)]
            for eng in ENGINES:
                C.op(eng, None, waits=snap)

        C.op("vector", "memset", ones128[:], 1.0)
        C.op("vector", "memset", epsv[:], 1e-5)
        C.op("vector", "memset", qtile[:], 0.25)
        C.op("vector", "memset", onescol[:], 1.0, inc=sV)

        C.op("gpsimd", "dma_start", out=cvt[:],
             in_=cv.rearrange("(c p) -> p c", p=128),
             allow_slow_non_contiguous=True, inc=smCV, incn=16)
        for j in range(16):
            C.op("gpsimd", "dma_start",
                 out=wbig.rearrange("p (c n) -> p c n", c=8)[:, :, j * 512:(j + 1) * 512],
                 in_=KTr[:, :, j * 512:(j + 1) * 512], inc=smKT, incn=16)

        C.op("sync", "dma_start", out=i8[:], in_=idf, inc=smMISC, incn=16)
        C.op("sync", "dma_start", out=i128[:], in_=idb, inc=smMISC, incn=16)
        C.op("sync", "dma_start", out=selt[:], in_=selb, inc=smMISC, incn=16)

        C.op("scalar", "activation", scT[:], cvt[:], AF.Silu,
             waits=[(smCV, 16)], inc=sS)
        silu_c_done = sS.n

        wmodr = wmod.rearrange("(c p) n -> p c n", p=128)

        def emit_wm_load(j):
            C.op("sync", "dma_start",
                 out=wmslot[j % 2].rearrange("p (c n) -> p c n", c=8),
                 in_=wmodr[:, :, j * 256:(j + 1) * 256],
                 waits=[(sT, modmm[j - 2] if j >= 2 else 0)],
                 inc=smWM, incn=16)

        def emit_wm_mms(j):
            extra = [(sV, mod1_done[0])] if j == 12 else []
            for jl in range(2):
                col = j * 2 + jl
                for dc in range(8):
                    first = not mod_started[0]
                    mod_started[0] = True
                    C.op("tensor", "matmul", pb[6][:, col:col + 1],
                         wmslot[j % 2][:, dc * 256 + jl * 128:
                                       dc * 256 + (jl + 1) * 128],
                         scT[:, dc:dc + 1],
                         start=first, stop=(j == 23 and jl == 1 and dc == 7),
                         waits=[(smWM, 16 * (j + 1)), (sS, silu_c_done)] + extra,
                         inc=(sT if dc == 7 else None))
                modmm[j] = sT.n

        for j in range(12):
            emit_wm_load(j)
            emit_wm_mms(j)

        def emit_mod_evac(part):
            lo = 0 if part == 0 else 24
            gg = 16 if part == 0 else 40
            cc = 8 if part == 0 else 32
            mm = modmm[11 if part == 0 else 23]
            C.op("vector", "tensor_copy", modv[:, lo:lo + 8],
                 pb[6][:, lo:lo + 8], waits=[(sT, mm)])
            C.op("vector", "tensor_scalar", modv[:, cc:cc + 8],
                 pb[6][:, cc:cc + 8], 1.0, None, ALU.add)
            C.op("vector", "tensor_scalar", csc[:, 0:8],
                 pb[6][:, gg:gg + 8], 0.0, None, ALU.is_ge, inc=sV)
            C.op("scalar", "activation", csc[:, 8:16], pb[6][:, gg:gg + 8],
                 AF.Abs, waits=[(sT, mm), (sV, sV.n)], inc=sS)
            C.op("vector", "tensor_tensor", csc[:, 8:16], csc[:, 8:16],
                 qtile[:], ALU.max, waits=[(sS, sS.n)])
            C.op("vector", "tensor_tensor", modv[:, gg:gg + 8], csc[:, 0:8],
                 csc[:, 8:16], ALU.mult)
            C.op("vector", "scalar_tensor_tensor", modv[:, gg:gg + 8],
                 modv[:, gg:gg + 8], 2.0, csc[:, 8:16],
                 ALU.mult, ALU.subtract, inc=sV)
            if part == 0:
                mod1_done[0] = sV.n

        emit_mod_evac(0)

        def preA_thunks(g):
            th = []
            xh = lambda dc: xh_chunk(0, dc)
            xload = [0]

            def t_load():
                C.op("sync", "dma_start",
                     out=xg[:, 0:4096].rearrange("p (c t) -> p c t", c=8),
                     in_=xTr[:, :, g * TG:(g + 1) * TG],
                     waits=[(sV, hfin[g - 1] if g >= 1 else 0)],
                     inc=smX, incn=16)
                xload[0] = smX.n
            th.append(t_load)

            def mk_cast(dc):
                def f():
                    C.op("vector", "tensor_copy", sustg(dc), xh(dc),
                         waits=[(smX, xload[0]),
                                (sT, sumMM.get(("A", g, dc - 2), 0))],
                         inc=sV)
                    cn = sV.n
                    w = [(sV, cn)]
                    if dc == 0 and g >= 1:
                        w.append((sV, rowsdone[("A", g - 1)]))
                    C.op("tensor", "matmul", pb[2][0:1, 0:512], ones128[:],
                         sustg(dc), start=(dc == 0), stop=(dc == 7),
                         waits=w, inc=sT)
                    sumMM[("A", g, dc)] = sT.n
                return f
            for dc in range(8):
                th.append(mk_cast(dc))

            def mk_sq(dc):
                def f():
                    C.op("scalar", "activation", sustg(dc), xh(dc), AF.Square,
                         waits=[(sT, sumMM[("A", g, 7)] if dc < 2
                                 else sumsqMM[("A", g, dc - 2)])],
                         inc=sS)
                    sn = sS.n
                    w = [(sS, sn)]
                    if dc == 0 and g >= 1:
                        w.append((sV, rowsdone[("A", g - 1)]))
                    C.op("tensor", "matmul", pb[3][0:1, 0:512], ones128[:],
                         sustg(dc), start=(dc == 0), stop=(dc == 7),
                         waits=w, inc=sT)
                    sumsqMM[("A", g, dc)] = sT.n
                return f
            for dc in range(8):
                th.append(mk_sq(dc))

            def t_rows():
                C.op("vector", "tensor_scalar", rowB1, pb[2][0:1, 0:512],
                     1.0 / D, None, ALU.mult,
                     waits=[(sT, sumMM[("A", g, 7)]),
                            (sT, cbcMM[g - 2] if g >= 2 else 0)])
                C.op("vector", "tensor_scalar", rowF1, pb[3][0:1, 0:512],
                     1.0 / D, None, ALU.mult,
                     waits=[(sT, sumsqMM[("A", g, 7)]),
                            (sS, cchainSqrt[g - 2] if g >= 2 else 0)])
                C.op("vector", "tensor_tensor", rowF2, rowB1, rowB1, ALU.mult)
                C.op("vector", "tensor_tensor", rowF1, rowF1, rowF2,
                     ALU.subtract, inc=sV)
                C.op("scalar", "activation", rowF2, rowF1, AF.Sqrt, bias=epsv[0:1, 0:1],
                     waits=[(sV, sV.n)], inc=sS)
                sqrtLN[("A", g)] = sS.n
                C.op("vector", "reciprocal", rowF1, rowF2,
                     waits=[(sS, sS.n)])
                C.op("vector", "tensor_tensor", rowB2, rowF1, rowF1,
                     ALU.max, inc=sV)
                rowsdone[("A", g)] = sV.n
            th.append(t_rows)

            def t_bc1():
                C.op("tensor", "matmul", pb[5][:, 0:512], onescol[:], rowB1,
                     start=True, stop=True,
                     waits=[(sV, rowsdone[("A", g)]),
                            (sV, cbcEv[g - 1] if g >= 1 else 0)], inc=sT)
                mubcMM[("A", g)] = sT.n
                C.op("vector", "tensor_copy", su[:, 0:512], pb[5][:, 0:512],
                     waits=[(sT, sT.n), (sT, sumsqMM[("A", g, 7)])], inc=sV)
                mubcEv[("A", g)] = sV.n
            th.append(t_bc1)

            def t_bc2():
                C.op("tensor", "matmul", pb[5][:, 0:512], onescol[:], rowB2,
                     start=True, stop=True,
                     waits=[(sV, mubcEv[("A", g)])], inc=sT)
                rsbcMM[("A", g)] = sT.n
                C.op("vector", "tensor_copy", su[:, 512:1024], pb[5][:, 0:512],
                     waits=[(sT, sT.n)], inc=sV)
                rsbcEv[("A", g)] = sV.n
            th.append(t_bc2)

            def mk_norm(dc0):
                def f():
                    for dc in (dc0, dc0 + 1):
                        C.op("vector", "tensor_tensor", xh(dc), xh(dc),
                             su[:, 0:512], ALU.subtract,
                             waits=[(sV, rsbcEv[("A", g)])])
                        C.op("vector", "tensor_tensor", xh(dc), xh(dc),
                             su[:, 512:1024], ALU.mult,
                             inc=(sV if dc == 7 else None))
                return f
            for dc0 in (0, 2, 4, 6):
                th.append(mk_norm(dc0))
            return th

        def emit_hfinA(g):
            for dc in range(8):
                C.op("vector", "tensor_scalar", hg_chunk(dc), xh_chunk(0, dc),
                     ca1_c(dc), sa_c(dc), ALU.mult, ALU.add,
                     waits=[(sV, mod1_done[0]),
                            (sT, mmA[g - 1][NCH - 1] if g >= 1 else 0)],
                     inc=(sV if dc == 7 else None))
            hfin[g] = sV.n

        def emit_sweepA(g, hooks):
            hooks = list(hooks)
            for k in range(NCH):
                pbk = pb[k % 2]
                for dc in range(8):
                    w = []
                    if dc == 0:
                        w.append((sV, hfin[g]))
                        if g == 0:
                            w.append((smKT, 16 * (k // 4 + 1)))
                        if k >= 2:
                            w += [(sS, sqA[g][k - 2]), (sV, rawA[g][k - 2])]
                        elif g >= 1:
                            w += [(sS, sqA[g - 1][NCH - 2 + k]),
                                  (sV, rawA[g - 1][NCH - 2 + k])]
                    C.op("tensor", "matmul", pbk[:, 0:512], KTt(dc, k),
                         hg_chunk(dc), start=(dc == 0), stop=(dc == 7),
                         waits=w, inc=(sT if dc == 7 else None))
                mmA[g][k] = sT.n
                if k >= 1:
                    kk = k - 1
                    w = [(sS, sqA[g][kk])]
                    if kk == 0 and g >= 1:
                        w.append((sV, cchainRecip[g - 1]))
                    C.op("tensor", "matmul", pb[4][0:1, 0:512], ones128[:],
                         S2s(kk), start=(kk == 0), stop=False,
                         waits=w, inc=sT)
                    onesA[g][kk] = sT.n
                w = [(sT, mmA[g][k])]
                if k >= 2:
                    w.append((sT, onesA[g][k - 2]))
                elif g >= 1:
                    w.append((sT, onesA[g - 1][NCH - 2 + k]))
                C.op("scalar", "activation", S2s(k), pbk[:, 0:512], AF.Square,
                     waits=w, inc=sS)
                sqA[g][k] = sS.n
                u = k // SU
                w = [(sT, mmA[g][k]), (sS, sqA[g][k])]
                if k % SU == 0:
                    gu = g * NU + u
                    if gu >= 2:
                        w.append((smSP, 16 * (gu - 1)))
                C.op("vector", "tensor_copy",
                     sbS[:, (u % 2) * 2048 + (k % SU) * 512:
                         (u % 2) * 2048 + (k % SU + 1) * 512],
                     pbk[:, 0:512], waits=w, inc=sV)
                rawA[g][k] = sV.n
                if k % SU == SU - 1:
                    C.op("sync", "dma_start",
                         out=Sdram[:, g, u * SU:(u + 1) * SU, :],
                         in_=sbS[:, (u % 2) * 2048:(u % 2 + 1) * 2048].rearrange(
                             "p (c t) -> p c t", c=SU),
                         waits=[(sV, rawA[g][k])], inc=smSP, incn=16)
                    spillN[0] += 1
                if hooks and k >= 2:
                    hooks.pop(0)()
            kk = NCH - 1
            C.op("tensor", "matmul", pb[4][0:1, 0:512], ones128[:], S2s(kk),
                 start=False, stop=True, waits=[(sS, sqA[g][kk])], inc=sT)
            onesA[g][kk] = sT.n
            while hooks:
                hooks.pop(0)()
            C.op("vector", "reciprocal", rowF1, pb[4][0:1, 0:512],
                 waits=[(sT, onesA[g][kk]),
                        (sS, sqrtLN.get(("A", g + 1), 0))], inc=sV)
            cchainRecip[g] = sV.n
            C.op("scalar", "activation", rowB1, rowF1, AF.Sqrt,
                 scale=float(NB),
                 waits=[(sV, cchainRecip[g]),
                        (sT, mubcMM.get(("A", g + 1), 0))], inc=sS)
            cchainSqrt[g] = sS.n
            C.op("tensor", "matmul", pb[5][:, 0:512], onescol[:], rowB1,
                 start=True, stop=True,
                 waits=[(sS, cchainSqrt[g]),
                        (sV, rsbcEv.get(("A", g + 1), 0))], inc=sT)
            cbcMM[g] = sT.n
            C.op("vector", "tensor_copy", scb[:, g * 512:(g + 1) * 512],
                 pb[5][:, 0:512], waits=[(sT, cbcMM[g])], inc=sV)
            cbcEv[g] = sV.n

        for f in preA_thunks(0):
            f()
        emit_hfinA(0)
        if phases == "P":
            for eng in ENGINES:
                C.op(eng, None, waits=[(sV, hfin[0]), (sT, sT.n), (sS, sS.n)])
            if _DEBUG_IO:
                C.op("sync", "dma_start", out=dbgM, in_=modv[:], inc=smMISC, incn=16)
                C.op("sync", "dma_start", out=dbgH, in_=hg[:], inc=smMISC, incn=16)
                C.op("sync", None, waits=[(smMISC, smMISC.n)])
            return _replay(nc, C)
        if _DEBUG_IO:
            C.op("sync", "dma_start", out=dbgH,
                 in_=hg[:], waits=[(sV, hfin[0])], inc=smMISC, incn=16)
            C.op("sync", None, waits=[(smMISC, smMISC.n)])
        g0_hooks = []
        for j in range(12, 24):
            g0_hooks.append(lambda j=j: emit_wm_load(j))
            g0_hooks.append(lambda j=j: emit_wm_mms(j))
        g0_hooks.append(lambda: emit_mod_evac(1))
        g0_hooks += preA_thunks(1)
        emit_sweepA(0, g0_hooks)
        emit_hfinA(1)
        if phases == "A1":
            snap = [(sT, sT.n), (sV, sV.n), (sS, sS.n), (smSP, 16 * spillN[0])]
            for eng in ENGINES:
                C.op(eng, None, waits=snap)
            return _replay(nc, C)
        for g in range(1, NG):
            if g + 1 < NG:
                hooks = preA_thunks(g + 1)
            else:
                def bx0():
                    C.op("gpsimd", "dma_start",
                         out=xg[:, 0:4096].rearrange("p (c t) -> p c t", c=8),
                         in_=xTr[:, :, 0:TG],
                         waits=[(sV, hfin[NG - 1])], inc=smXB, incn=16)
                hooks = [bx0]
            emit_sweepA(g, hooks)
            if g + 1 < NG:
                emit_hfinA(g + 1)

        bx_load = smXB.n
        phase_barrier()

        if phases == "A":
            C.op("sync", None, waits=[(smSP, 16 * spillN[0])])
            if _DEBUG_IO:
                C.op("sync", "dma_start", out=dbgM, in_=modv[:],
                     waits=[(sV, sV.n), (sS, sS.n), (sT, sT.n)],
                     inc=smMISC, incn=16)
                C.op("sync", None, waits=[(smMISC, smMISC.n)])
            return _replay(nc, C)

        for vc in range(8):
            C.op("gpsimd", "dma_start",
                 out=wbig[:, vc * 8192:(vc + 1) * 8192].rearrange(
                     "p (c d) -> p c d", c=8),
                 in_=Vr[:, vc * 8:(vc + 1) * 8, :],
                 waits=[(sT, mmA[NG - 1][NCH - 1])], inc=smV, incn=16)

        bxof = [0] * NG
        bxof[0] = bx_load
        for g in range(NG):
            if g >= 1:
                C.op("gpsimd", "dma_start",
                     out=xg[:, 0:4096].rearrange("p (c t) -> p c t", c=8),
                     in_=xTr[:, :, g * TG:(g + 1) * TG],
                     waits=[(smX2, 16 * g)], inc=smXB, incn=16)
                bxof[g] = smXB.n
            for u in range(NU):
                gu = g * NU + u
                w = [(smSP, 16 * spillN[0])]
                if gu >= 2:
                    w.append((sT, mmBc[(gu - 2) * SU + SU - 1]))
                C.op("sync", "dma_start",
                     out=sbS[:, (u % 2) * 2048:(u % 2 + 1) * 2048].rearrange(
                         "p (c t) -> p c t", c=SU),
                     in_=Sdram[:, g, u * SU:(u + 1) * SU, :],
                     waits=w, inc=smSI, incn=16)
                si_n = smSI.n
                for cc in range(SU):
                    k = u * SU + cc
                    kg = g * NCH + k
                    sl = sbS[:, (u % 2) * 2048 + cc * 512:
                             (u % 2) * 2048 + (cc + 1) * 512]
                    C.op("vector", "tensor_tensor", sl, sl,
                         scb[:, g * 512:(g + 1) * 512], ALU.mult,
                         waits=[(smSI, si_n), (sV, cbcEv[g])], inc=sV)
                    C.op("scalar", "activation", sl, sl, AF.Gelu,
                         waits=[(sV, sV.n)], inc=sS)
                    gel_n = sS.n
                    for dc in range(8):
                        w = [(sS, gel_n)]
                        if k == 0 and g >= 1:
                            w.append((sV, oev[g - 1][dc]))
                        if g == 0:
                            w.append((smV, 16 * (k // 8 + 1)))
                        C.op("tensor", "matmul", pb[dc][:, 0:512], Vt(k, dc),
                             sl, start=(k == 0), stop=(k == NCH - 1),
                             waits=w, inc=(sT if dc == 7 else None))
                    mmBc[kg] = sT.n
            for dc in range(8):
                C.op("vector", "scalar_tensor_tensor", xh_chunk(0, dc),
                     pb[dc][:, 0:512], ga_c(dc), xh_chunk(0, dc),
                     ALU.mult, ALU.add,
                     waits=[(sT, mmBc[g * NCH + NCH - 1]), (smXB, bxof[g])],
                     inc=sV)
                oev[g][dc] = sV.n
            C.op("gpsimd", "dma_start", out=X2dram[:, g, :, :],
                 in_=xg[:, 0:4096].rearrange("p (c t) -> p c t", c=8),
                 waits=[(sV, oev[g][7])], inc=smX2, incn=16)

        b_last = mmBc[NG * NCH - 1]
        phase_barrier()

        if phases == "B":
            C.op("sync", None, waits=[(smX2, 16 * NG)])
            return _replay(nc, C)

        g7 = (NG - 1) * NCH
        C.op("gpsimd", "dma_start", out=gwb[:],
             in_=gw.rearrange("(c p) e -> p c e", p=128),
             allow_slow_non_contiguous=True,
             waits=[(sT, mmBc[g7 + 15])], inc=smW, incn=16)
        for e in range(E):
            C.op("gpsimd", "dma_start",
                 out=wbig[:, e * 2048:(e + 1) * 2048].rearrange(
                     "p (c r) -> p c r", c=8),
                 in_=Wu[e].rearrange("(c p) r -> p c r", p=128),
                 waits=[(sT, mmBc[g7 + 15])], inc=smW, incn=16)
        for e in range(E):
            C.op("gpsimd", "dma_start",
                 out=wbig[:, 16384 + e * 2048: 16384 + (e + 1) * 2048].rearrange(
                     "p (c r) -> p c r", c=8),
                 in_=Wv[e].rearrange("(c p) r -> p c r", p=128),
                 waits=[(sT, mmBc[g7 + 31])], inc=smW, incn=16)
        for e in range(E):
            C.op("gpsimd", "dma_start",
                 out=wbig[:, 32768 + e * 2048: 32768 + (e + 1) * 2048].rearrange(
                     "p (c d) -> p c d", c=2),
                 in_=Wo[e].rearrange("(c p) d -> p c d", p=128),
                 waits=[(sT, mmBc[g7 + 47])], inc=smW, incn=16)
        smW_all = smW.n

        def preC_thunks(g):
            th = []
            half = (g + 1) % 2
            xh = lambda dc: xh_chunk(half, dc)
            xload = [0]

            def t_load():
                w = [(smX2, 16 * (g + 1))]
                if g >= 2:
                    w.append((smY, 16 * (g - 1)))
                C.op("sync", "dma_start",
                     out=xg[:, half * 4096: half * 4096 + 4096].rearrange(
                         "p (c t) -> p c t", c=8),
                     in_=X2dram[:, g, :, :], waits=w, inc=smXC, incn=16)
                xload[0] = smXC.n
            th.append(t_load)

            def mk_cast(dc):
                def f():
                    w = [(smXC, xload[0]),
                         (sT, sumMM.get(("C", g, dc - 2), 0))]
                    if dc < 2 and g >= 1:
                        w.append((sT, sumsqMM[("C", g - 1, 6 + dc)]))
                    C.op("vector", "tensor_copy", sbstg(dc), xh(dc),
                         waits=w, inc=sV)
                    cn = sV.n
                    w = [(sV, cn)]
                    if dc == 0 and g >= 1:
                        w += [(sV, rowsdone[("C", g - 1)]),
                              (sV, YevD[g - 1][6])]
                    C.op("tensor", "matmul", pb[6][0:1, 0:512], ones128[:],
                         sbstg(dc), start=(dc == 0), stop=(dc == 7),
                         waits=w, inc=sT)
                    sumMM[("C", g, dc)] = sT.n
                return f
            for dc in range(8):
                th.append(mk_cast(dc))

            def mk_sq(dc):
                def f():
                    C.op("scalar", "activation", sbstg(dc), xh(dc), AF.Square,
                         waits=[(sT, sumMM[("C", g, 7)] if dc < 2
                                 else sumsqMM[("C", g, dc - 2)])],
                         inc=sS)
                    sn = sS.n
                    w = [(sS, sn)]
                    if dc == 0 and g >= 1:
                        w.append((sV, YevD[g - 1][7]))
                    C.op("tensor", "matmul", pb[7][0:1, 0:512], ones128[:],
                         sbstg(dc), start=(dc == 0), stop=(dc == 7),
                         waits=w, inc=sT)
                    sumsqMM[("C", g, dc)] = sT.n
                return f
            for dc in range(8):
                th.append(mk_sq(dc))

            def t_rows():
                C.op("vector", "tensor_scalar", rowB1, pb[6][0:1, 0:512],
                     1.0 / D, None, ALU.mult,
                     waits=[(sT, sumMM[("C", g, 7)]),
                            (sT, mubcMM.get(("C", g - 1), 0))])
                C.op("vector", "tensor_scalar", rowF1, pb[7][0:1, 0:512],
                     1.0 / D, None, ALU.mult,
                     waits=[(sT, sumsqMM[("C", g, 7)]),
                            (sT, trlgD[g - 1][3] if g >= 1 else 0),
                            (sS, sqrtLN.get(("C", g - 1), 0))])
                C.op("vector", "tensor_tensor", rowF2, rowB1, rowB1, ALU.mult)
                C.op("vector", "tensor_tensor", rowF1, rowF1, rowF2,
                     ALU.subtract, inc=sV)
                C.op("scalar", "activation", rowF2, rowF1, AF.Sqrt, bias=epsv[0:1, 0:1],
                     waits=[(sV, sV.n)], inc=sS)
                sqrtLN[("C", g)] = sS.n
                C.op("vector", "reciprocal", rowF1, rowF2,
                     waits=[(sS, sS.n)])
                C.op("vector", "tensor_tensor", rowB2, rowF1, rowF1,
                     ALU.max, inc=sV)
                rowsdone[("C", g)] = sV.n
            th.append(t_rows)

            def t_bc1():
                C.op("tensor", "matmul", pb[4][:, 0:512], onescol[:], rowB1,
                     start=True, stop=True,
                     waits=[(sV, rowsdone[("C", g)]),
                            (sV, mtEv[g - 1][3] if g >= 1 else 0)], inc=sT)
                mubcMM[("C", g)] = sT.n
                C.op("vector", "tensor_copy", sbS[:, 1024:1536],
                     pb[4][:, 0:512], waits=[(sT, sT.n)], inc=sV)
                mubcEv[("C", g)] = sV.n
            th.append(t_bc1)

            def t_bc2():
                C.op("tensor", "matmul", pb[4][:, 0:512], onescol[:], rowB2,
                     start=True, stop=True,
                     waits=[(sV, mubcEv[("C", g)])], inc=sT)
                rsbcMM[("C", g)] = sT.n
                C.op("vector", "tensor_copy", sbS[:, 1536:2048],
                     pb[4][:, 0:512], waits=[(sT, sT.n)], inc=sV)
                rsbcEv[("C", g)] = sV.n
            th.append(t_bc2)

            return th

        def emit_h2fin(g):
            half = (g + 1) % 2
            for dc in range(8):
                w = [(sV, mod1_done[0]), (sV, rsbcEv[("C", g)])]
                if g >= 1:
                    w.append((sT, vmmD[g - 1][E - 1][1]))
                C.op("vector", "tensor_tensor", hg_chunk(dc),
                     xh_chunk(half, dc), sbS[:, 1024:1536], ALU.subtract,
                     waits=w)
                C.op("vector", "tensor_tensor", hg_chunk(dc), hg_chunk(dc),
                     sbS[:, 1536:2048], ALU.mult)
                C.op("vector", "tensor_scalar", hg_chunk(dc), hg_chunk(dc),
                     cb1_c(dc), sb_c(dc), ALU.mult, ALU.add,
                     inc=(sV if dc == 7 else None))
            h2fin[g] = sV.n

        def emit_groupC(g, hooks):
            hooks = list(hooks)
            half = (g + 1) % 2
            for dc in range(8):
                w = [(sV, h2fin[g]), (smW, 16)]
                if dc == 0 and g >= 1:
                    w.append((sS, siluD[g - 1][E - 1][0]))
                C.op("tensor", "matmul", pb[0][0:8, 0:512],
                     gwb[:, dc * 8:(dc + 1) * 8], hg_chunk(dc),
                     start=(dc == 0), stop=(dc == 7),
                     waits=w, inc=(sT if dc == 7 else None))
            glogD[g] = sT.n
            C.op("vector", "tensor_copy", lgT, pb[0][0:8, 0:512],
                 waits=[(sT, glogD[g]), (sS, sqrtLN[("C", g)]),
                        (sT, trlgD[g - 1][3] if g >= 1 else 0)], inc=sV)
            lgtEv[g] = sV.n
            gcpy = 0
            for sub in range(4):
                w = [(sV, lgtEv[g]), (smMISC, 48)]
                if sub >= 1:
                    w.append((sV, gcpy))
                elif g >= 1:
                    w.append((sV, mtEv[g - 1][3]))
                C.op("tensor", "transpose", pb[4][0:128, 0:8],
                     lgT[0:8, sub * 128:(sub + 1) * 128], i8[:],
                     waits=w, inc=sT)
                trlgD[g][sub] = sT.n
                C.op("vector", "tensor_copy", gtok[:, sub * 8:(sub + 1) * 8],
                     pb[4][:, 0:8],
                     waits=[(sT, trlgD[g][sub]),
                            (sT, trmD[g - 1][3] if (sub == 0 and g >= 1) else 0)],
                     inc=sV)
                gcpy = sV.n
            for rc in range(2):
                for dc in range(8):
                    w = [(smW, 16 * 2)]
                    if dc == 0:
                        if rc == 0:
                            w.append((sV, lgtEv[g]))
                            if g >= 1:
                                w.append((sS, siluD[g - 1][E - 1][0]))
                        elif g >= 1:
                            w.append((sS, siluD[g - 1][E - 1][1]))
                    C.op("tensor", "matmul", pb[rc][:, 0:512],
                         Wut(0, dc, rc), hg_chunk(dc),
                         start=(dc == 0), stop=(dc == 7),
                         waits=w, inc=(sT if dc == 7 else None))
                ummD[g][0][rc] = sT.n
                for dc in range(8):
                    w = [(smW, 16 * 10)]
                    if dc == 0 and g >= 1:
                        w.append((sV, sslD[g - 1][E - 1][rc]))
                    C.op("tensor", "matmul", pb[2 + rc][:, 0:512],
                         Wvt(0, dc, rc), hg_chunk(dc),
                         start=(dc == 0), stop=(dc == 7),
                         waits=w, inc=(sT if dc == 7 else None))
                vmmD[g][0][rc] = sT.n
            gts = lambda s: gtok[:, s * 8:(s + 1) * 8]
            tcs = gts
            mk0 = lambda s: msk[:, s * 8:(s + 1) * 8]
            mk1 = lambda s: msk[:, 32 + s * 8:32 + (s + 1) * 8]
            tp = lambda s, i: tops[:, s * 8 + i:s * 8 + i + 1]
            mbs = lambda s: m_bf[:, s * 8:(s + 1) * 8]
            for s in range(4):
                C.op("vector", "reduce_max", tp(s, 0), gts(s), axis=AX.X)
            for s in range(4):
                C.op("vector", "tensor_scalar", mk0(s), gts(s), tp(s, 0),
                     None, ALU.is_equal)
            for s in range(4):
                C.op("vector", "scalar_tensor_tensor", tcs(s), mk0(s), -1e30,
                     gts(s), ALU.mult, ALU.add)
            for s in range(4):
                C.op("vector", "reduce_max", tp(s, 1), tcs(s), axis=AX.X)
            for s in range(4):
                C.op("vector", "tensor_scalar", mk1(s), tcs(s), tp(s, 1),
                     None, ALU.is_equal)
            for s in range(4):
                C.op("vector", "tensor_tensor", tp(s, 2), tp(s, 0), tp(s, 1),
                     ALU.subtract, inc=(sV if s == 3 else None))
            for s in range(4):
                C.op("scalar", "activation", tp(s, 3), tp(s, 2), AF.Sigmoid,
                     waits=[(sV, sV.n)], inc=(sS if s == 3 else None))
            for s in range(4):
                C.op("scalar", "activation", tp(s, 4), tp(s, 2), AF.Sigmoid,
                     scale=-1.0, inc=(sS if s == 3 else None))
            for s in range(4):
                C.op("vector", "tensor_scalar", mbs(s), mk0(s), tp(s, 3),
                     None, ALU.mult, waits=[(sS, sS.n)])
            for s in range(4):
                C.op("vector", "scalar_tensor_tensor", mbs(s), mk1(s),
                     tp(s, 4), mbs(s), ALU.mult, ALU.add,
                     inc=(sV if s == 3 else None))
            mdone = sV.n
            for sub in range(4):
                w = [(sV, mdone)]
                if sub >= 1:
                    w.append((sV, mtEv[g][sub - 1]))
                C.op("tensor", "transpose", pb[4][0:8, 0:128], mbs(sub),
                     i128[:], waits=w, inc=sT)
                trmD[g][sub] = sT.n
                w = [(sT, trmD[g][sub])]
                if sub == 0 and g >= 1:
                    w.append((sT, mbcMMn[g - 1][E - 1]))
                C.op("vector", "tensor_copy",
                     mT[0:8, sub * 128:(sub + 1) * 128], pb[4][0:8, 0:128],
                     waits=w, inc=sV)
                mtEv[g][sub] = sV.n
            for e in range(E):
                w = [(sV, mtEv[g][3]), (smMISC, 48)]
                if e >= 1:
                    w.append((sV, mbcEv[g][e - 1]))
                elif g >= 1:
                    w.append((sV, mbcEv[g - 1][E - 1]))
                C.op("tensor", "matmul", pb[5][:, 0:512],
                     selt[0:8, e * 128:(e + 1) * 128], mT[0:8, 0:512],
                     start=True, stop=True, waits=w, inc=sT)
                mbcMMn[g][e] = sT.n
                C.op("vector", "tensor_copy", m_bc[:], pb[5][:, 0:512],
                     waits=[(sT, mbcMMn[g][e]),
                            (sV, sslD[g][e - 1][1] if e >= 1 else
                             (sslD[g - 1][E - 1][1] if g >= 1 else 0))],
                     inc=sV)
                mbcEv[g][e] = sV.n
                for rc in range(2):
                    if e >= 1:
                        for dc in range(8):
                            w = [(smW, 16 * (2 + e))]
                            if dc == 0:
                                w.append((sS, siluD[g][e - 1][rc]))
                            C.op("tensor", "matmul", pb[rc][:, 0:512],
                                 Wut(e, dc, rc), hg_chunk(dc),
                                 start=(dc == 0), stop=(dc == 7),
                                 waits=w, inc=(sT if dc == 7 else None))
                        ummD[g][e][rc] = sT.n
                        for dc in range(8):
                            w = [(smW, 16 * (10 + e))]
                            if dc == 0:
                                w.append((sV, sslD[g][e - 1][rc]))
                            C.op("tensor", "matmul", pb[2 + rc][:, 0:512],
                                 Wvt(e, dc, rc), hg_chunk(dc),
                                 start=(dc == 0), stop=(dc == 7),
                                 waits=w, inc=(sT if dc == 7 else None))
                        vmmD[g][e][rc] = sT.n
                    w = [(sT, ummD[g][e][rc])]
                    if e >= 1:
                        w.append((sV, sslD[g][e - 1][rc]))
                    elif g >= 1:
                        w.append((sV, sslD[g - 1][E - 1][rc]))
                    C.op("scalar", "activation", sustg(rc), pb[rc][:, 0:512],
                         AF.Silu, waits=w, inc=sS)
                    siluD[g][e][rc] = sS.n
                    C.op("vector", "tensor_tensor", sslot(e, rc), sustg(rc),
                         m_bc[:], ALU.mult,
                         waits=[(sS, siluD[g][e][rc]),
                                (sV, mbcEv[g][e]),
                                (sT, YmmD[g - 1][7] if g >= 1 else 0)])
                    C.op("vector", "tensor_tensor", sslot(e, rc),
                         sslot(e, rc), pb[2 + rc][:, 0:512], ALU.mult,
                         waits=[(sT, vmmD[g][e][rc])], inc=sV)
                    sslD[g][e][rc] = sV.n
                for _ in range(3):
                    if hooks:
                        hooks.pop(0)()
            while hooks:
                hooks.pop(0)()
            if g + 1 < NG and ("C", g + 1) in rsbcEv:
                emit_h2fin(g + 1)
            for dc in range(8):
                pby = pb[6 + dc % 2]
                for i, (e, rc) in enumerate((e, rc) for e in range(E)
                                            for rc in range(2)):
                    w = []
                    if i == 0:
                        w = [(sV, sslD[g][E - 1][1]), (smW, 16 * 25)]
                        if dc >= 2:
                            w.append((sV, YevD[g][dc - 2]))
                        elif g + 1 < NG:
                            w.append((sV, rowsdone.get(("C", g + 1), 0)))
                    C.op("tensor", "matmul", pby[:, 0:512], Wot(e, rc, dc),
                         sslot(e, rc), start=(i == 0), stop=(i == 15),
                         waits=w, inc=(sT if i == 15 else None))
                YmmD[g][dc] = sT.n
                C.op("vector", "scalar_tensor_tensor", xh_chunk(half, dc),
                     pby[:, 0:512], gb_c(dc), xh_chunk(half, dc),
                     ALU.mult, ALU.add, waits=[(sT, YmmD[g][dc])], inc=sV)
                YevD[g][dc] = sV.n
            C.op("sync", "dma_start", out=yTr[:, :, g * TG:(g + 1) * TG],
                 in_=xg[:, half * 4096: half * 4096 + 4096].rearrange(
                     "p (c t) -> p c t", c=8),
                 waits=[(sV, YevD[g][7])], inc=smY, incn=16)

        for f in preC_thunks(0):
            f()
        emit_h2fin(0)
        if phases == "C1":
            emit_groupC(0, [])
            snap = [(sT, sT.n), (sV, sV.n), (sS, sS.n)]
            for eng in ENGINES:
                C.op(eng, None, waits=snap)
            if _DEBUG_IO:
                C.op("sync", "dma_start", out=dbgH, in_=hg[:], inc=smMISC, incn=16)
                C.op("sync", "dma_start", out=dbgC, in_=scb[:], inc=smMISC, incn=16)
                C.op("sync", "dma_start", out=dbgT, in_=mT[:], inc=smMISC, incn=16)
                C.op("sync", "dma_start", out=dbgB, in_=m_bc[:], inc=smMISC, incn=16)
                C.op("sync", "dma_start", out=dbgS, in_=sbS[:], inc=smMISC, incn=16)
                C.op("sync", "dma_start", out=dbgX, in_=xg[:], inc=smMISC, incn=16)
                C.op("sync", "dma_start", out=dbgM, in_=modv[:], inc=smMISC, incn=16)
                C.op("sync", None, waits=[(smMISC, smMISC.n)])
            C.op("sync", None, waits=[(smY, 16 * 1)])
            return _replay(nc, C)
        for g in range(NG):
            hooks = preC_thunks(g + 1) if g + 1 < NG else []
            emit_groupC(g, hooks)

        if _DEBUG_IO:
            C.op("sync", "dma_start", out=dbgM, in_=modv[:],
                 waits=[(sV, sV.n), (sS, sS.n), (sT, sT.n)], inc=smMISC, incn=16)
            C.op("sync", "dma_start", out=dbgH, in_=hg[:], inc=smMISC, incn=16)
            C.op("sync", "dma_start", out=dbgT, in_=mT[:], inc=smMISC, incn=16)
            C.op("sync", "dma_start", out=dbgC, in_=scb[:],
                 waits=[(smMISC, smMISC.n), (sV, sV.n)], inc=smMISC, incn=16)
            C.op("sync", None, waits=[(smMISC, smMISC.n)])
        C.op("sync", None, waits=[(smY, 16 * NG)])
        return _replay(nc, C)


def _replay(nc, C):
    with nc.Block() as block:
        @block.tensor
        def _(eng):
            for f in C.prog["tensor"]:
                f(eng)

        @block.vector
        def _(eng):
            for f in C.prog["vector"]:
                f(eng)

        @block.scalar
        def _(eng):
            for f in C.prog["scalar"]:
                f(eng)

        @block.sync
        def _(eng):
            for f in C.prog["sync"]:
                f(eng)

        @block.gpsimd
        def _(eng):
            for f in C.prog["gpsimd"]:
                f(eng)
    return nc


_NC_CACHE = []


def kernel(L0_attn_q, L0_attn_k, L0_attn_v, L0_ffn_gate_w, L0_ffn_Wu,
           L0_ffn_Wv, L0_ffn_Wo, L0_c, L0_wmod):
    x = np.asarray(L0_attn_q, dtype=np.float32)
    K = np.asarray(L0_attn_k, dtype=np.float32)
    Vv = np.ascontiguousarray(np.asarray(L0_attn_v, dtype=np.float32))
    gwa = np.ascontiguousarray(np.asarray(L0_ffn_gate_w, dtype=np.float32))
    Wua = np.ascontiguousarray(np.asarray(L0_ffn_Wu, dtype=np.float32))
    Wva = np.ascontiguousarray(np.asarray(L0_ffn_Wv, dtype=np.float32))
    Woa = np.ascontiguousarray(np.asarray(L0_ffn_Wo, dtype=np.float32))
    Ca = np.asarray(L0_c, dtype=np.float32)
    wmoda = np.ascontiguousarray(np.asarray(L0_wmod, dtype=np.float32))
    KTa = np.ascontiguousarray(K.T)
    idf = np.eye(8, dtype=np.float32)
    idb = np.eye(128, dtype=np.float32)
    sel = np.zeros((8, 1024), dtype=ml_dtypes.bfloat16)
    for e in range(8):
        sel[e, e * 128:(e + 1) * 128] = 1

    if not _NC_CACHE:
        _NC_CACHE.append(build_nc())
    nc = _NC_CACHE[0]

    in_maps = []
    for b in range(8):
        in_maps.append({
            "xT": np.ascontiguousarray(x[b].T),
            "KT": KTa, "V": Vv, "Wu": Wua, "Wv": Wva, "Wo": Woa,
            "gw": gwa, "wmod": wmoda, "cv": np.ascontiguousarray(Ca[b]),
            "idf": idf, "idb": idb, "selb": sel,
        })
    res = run_bass_kernel_spmd(nc, in_maps, core_ids=list(range(8)))
    out = np.empty((8, 4096, 1024), np.float32)
    for b in range(8):
        out[b] = np.asarray(res.results[b]["yT"]).T
    return out
